# version 03 (final)
# baseline (speedup 1.0000x reference)
import numpy as np
from collections import defaultdict
from contextlib import ExitStack

import concourse.bass as bass
import concourse.mybir as mybir
from concourse.bass_utils import run_bass_kernel_spmd

F32 = mybir.dt.float32
BF16 = mybir.dt.bfloat16
AF = mybir.ActivationFunctionType
ALU = mybir.AluOpType
AX = mybir.AxisListType

T = 2048
DM = 2048
NT = 16
HQ, HKV, HD = 16, 4, 128
FD = 768
NFC = 6
NU = 3
EPS_LN = 1e-5
EPS_AT = 1e-8
NE, IE = 16, 1408
ISH = 5632
MNP = 129

DEBUG_TAPS = {}

ENGINES = ("pe", "dve", "act", "pool", "sp")


class Ev(tuple):
    __slots__ = ()


class Buf:
    __slots__ = ("ready", "readers")

    def __init__(self):
        self.ready = None
        self.readers = []


class Sched:
    def __init__(self):
        self.items = {e: [] for e in ENGINES}
        self.cnt = defaultdict(int)
        self.waited = {e: defaultdict(int) for e in ENGINES}
        self.sem_names = set()
        self.eng_sem = {e: "s_" + e for e in ENGINES}
        for e in ENGINES:
            self.sem_names.add("s_" + e)
        self.floor = []

    def _wait(self, eng, ev):
        if ev is None:
            return
        s, v = ev
        if self.waited[eng][s] >= v:
            return
        self.waited[eng][s] = v
        self.items[eng].append(("wait", s, v))

    def op(self, eng, fn, deps=(), inc=False, sem=None, ninc=1):
        for d in self.floor:
            self._wait(eng, d)
        for d in deps:
            self._wait(eng, d)
        ev = None
        if inc:
            s = sem or self.eng_sem[eng]
            self.sem_names.add(s)
            self.cnt[s] += ninc
            ev = Ev((s, self.cnt[s]))
            self.items[eng].append(("emit", fn, s, ninc))
        else:
            self.items[eng].append(("emit", fn, None, 0))
        return ev

    def dma(self, eng, fn, deps=(), sem="dma0"):
        return self.op(eng, fn, deps=deps, inc=True, sem=sem, ninc=16)

    @staticmethod
    def rd(*bufs):
        return [b.ready for b in bufs if b.ready is not None]

    @staticmethod
    def wr(*bufs):
        deps = []
        for b in bufs:
            deps += b.readers
            if b.ready is not None:
                deps.append(b.ready)
        return deps

    @staticmethod
    def set_ready(buf, ev):
        buf.ready = ev
        buf.readers = []

    @staticmethod
    def add_reader(buf, ev):
        if ev is not None:
            buf.readers.append(ev)

    def snapshot(self):
        return [Ev((s, self.cnt[s])) for s in list(self.cnt) if self.cnt[s] > 0]

    def set_floor(self):
        self.floor = self.snapshot()


def build_program():
    nc = bass.Bass("TRN2", target_bir_lowering=False)
    sch = Sched()

    def din(name, shape, dtype=F32):
        return nc.dram_tensor(name, list(shape), dtype, kind="ExternalInput").ap()

    x_hbm = din("x", [T, DM])
    wq_hbm = din("wq", [16, 128, DM])
    wk_hbm = din("wk", [16, 128, 512])
    wv_hbm = din("wv", [16, 128, 512])
    wg_hbm = din("wg", [16, 128, DM])
    wmb_hbm = din("wmb", [16, 128, DM])
    bias_hbm = din("biases", [1, 3 * DM + 2 * 512])
    mem_hbm = din("mem", [HQ, NFC, 128, HD])
    normf_hbm = din("normf", [HQ, FD])
    gatew_hbm = din("gatew", [16, 128, NE])
    sgatew_hbm = din("sgatew", [128, 16])
    w13_hbm = din("w13", [NE, 22, 128, 16, 128])
    w2_hbm = din("w2", [NE, 16, 128, 11, 128])
    sgu_hbm = din("sgu", [88, 128, 16, 128])
    sdw_hbm = din("sdw", [16, 128, 44, 128])

    out_hbm = nc.dram_tensor("out", [T, DM], F32, kind="ExternalOutput").ap()

    taps = {}
    for tname, tshape in DEBUG_TAPS.items():
        taps[tname] = nc.dram_tensor("tap_" + tname, list(tshape), F32,
                                     kind="ExternalOutput").ap()

    aout_dram = nc.dram_tensor("aout_scr", [T, DM], F32, kind="Internal").ap()
    x1_dram = nc.dram_tensor("x1_scr", [T, DM], F32, kind="Internal").ap()
    x2b_dram = nc.dram_tensor("x2b_scr", [8, 128, DM], BF16, kind="Internal").ap()
    DSZ = FD * HD + FD
    dAB_in = [nc.dram_tensor(f"d{i}_in", [8, DSZ], F32, kind="Internal").ap()
              for i in range(2)]
    dAB_out = [nc.dram_tensor(f"d{i}_out", [8, DSZ], F32, kind="Internal").ap()
               for i in range(2)]

    BQ0, BG0, BMB0, BK0, BV0 = 0, DM, 2 * DM, 3 * DM, 3 * DM + 512

    main_ctx = ExitStack()
    with main_ctx as ctx:
        E = ctx.enter_context

        def sb(name, shape, dtype, ectx=None):
            return (ectx or ctx).enter_context(nc.sbuf_tensor(name, list(shape), dtype))

        def psum(name, shape, dtype=F32):
            return E(nc.psum_tensor(name, list(shape), dtype))

        ident = sb("ident", [128, 128], BF16)
        identf = sb("identf", [128, 128], F32)
        ones_row = sb("ones_row", [1, 128], BF16)
        m1_sb = sb("m1_sb", [128, NT], F32)
        r1_sb = sb("r1_sb", [128, NT], F32)
        s2_sb = sb("s2_sb", [128, NT], F32)
        q2_sb = sb("q2_sb", [128, NT], F32)
        m2_sb = sb("m2_sb", [128, NT], F32)
        r2_sb = sb("r2_sb", [128, NT], F32)
        m3_sb = sb("m3_sb", [128, NT], F32)
        r3_sb = sb("r3_sb", [128, NT], F32)
        cols = sb("cols_scr", [128, 16], F32)
        CONSTS = [EPS_LN, 1.0 / DM, 1.0, -1.0, EPS_AT]
        const_sb = sb("const_sb", [128, len(CONSTS)], F32)
        for j, cv in enumerate(CONSTS):
            nc.const_aps.register(F32, cv, const_sb[:, j:j + 1])
        C_ONE = const_sb[:, 2:3]

        p512 = [psum(f"p512_{i}", [128, 512], F32) for i in range(6)]
        pb = [Buf() for _ in p512]
        tp_ps = [psum(f"tp{i}", [128, 1024], BF16) for i in range(2)]
        tp_b = [Buf() for _ in tp_ps]

        def init_consts():
            ev = sch.op("pool", lambda: nc.gpsimd.memset(ident[:], 0.0), inc=True)
            sch.op("pool", lambda: nc.gpsimd.affine_select(
                out=ident[:], in_=ident[:], compare_op=ALU.not_equal, fill=1.0,
                base=0, pattern=[[-1, 128]], channel_multiplier=1), deps=[ev])
            ev = sch.op("pool", lambda: nc.gpsimd.memset(identf[:], 0.0), inc=True)
            sch.op("pool", lambda: nc.gpsimd.affine_select(
                out=identf[:], in_=identf[:], compare_op=ALU.not_equal, fill=1.0,
                base=0, pattern=[[-1, 128]], channel_multiplier=1), deps=[ev])
            sch.op("pool", lambda: nc.gpsimd.memset(ones_row[:], 1.0))
            sch.op("pool", lambda: nc.gpsimd.memset(s2_sb[:], 0.0))
            sch.op("pool", lambda: nc.gpsimd.memset(q2_sb[:], 0.0))
            for j, cv in enumerate(CONSTS):
                sch.op("pool", (lambda j=j, cv=cv: lambda: nc.gpsimd.memset(
                    const_sb[:, j:j + 1], cv))())
            return sch.op("pool", lambda: nc.gpsimd.engine_nop() if False else
                          nc.gpsimd.memset(cols[:], 0.0), inc=True)

        consts_ev = init_consts()
        for _e in ENGINES:
            sch._wait(_e, consts_ev)

        def transpose_tiles(srcs, dsts, ev_deps, dtype="bf16", evac_eng_alt=True):
            outevs = []
            n = len(srcs)
            for g4 in range(0, n, 4):
                cnt = min(4, n - g4)
                if dtype == "bf16":
                    ts = (g4 // 4) % 2
                    bank, bankbuf, idn = tp_ps[ts], tp_b[ts], ident
                else:
                    ts = (g4 // 4) % 2
                    bank, bankbuf, idn = p512[ts], pb[ts], identf
                tev = None
                for j in range(cnt):
                    tev = sch.op("pe", (lambda s=srcs[g4 + j], bank=bank, j=j, idn=idn:
                                 lambda: nc.tensor.transpose(
                                     bank[:, j * 128:(j + 1) * 128], s, idn[:]))(),
                                 deps=(sch.wr(bankbuf) + list(ev_deps)) if j == 0 else (),
                                 inc=(j == cnt - 1))
                sch.set_ready(bankbuf, tev)
                eng = "act" if (not evac_eng_alt or (g4 // 4) % 2 == 0) else "dve"
                eev = None
                for j in range(cnt):
                    def mkf(d=dsts[g4 + j], bank=bank, j=j, eng=eng):
                        if eng == "act":
                            return lambda: nc.scalar.copy(d, bank[:, j * 128:(j + 1) * 128])
                        return lambda: nc.vector.tensor_copy(d, bank[:, j * 128:(j + 1) * 128])
                    eev = sch.op(eng, mkf(), deps=sch.rd(bankbuf) if j == 0 else (),
                                 inc=(j == cnt - 1))
                sch.add_reader(bankbuf, eev)
                outevs.append(eev)
            return outevs

        at_ctx = ExitStack()
        xin_fm = sb("xin_fm", [128, 16 * T], BF16, at_ctx)
        xt = [sb(f"xt{i}", [128, DM], F32, at_ctx) for i in range(2)]
        xt_b = [Buf() for _ in xt]
        xbf = [sb(f"xbf{i}", [128, DM], BF16, at_ctx) for i in range(2)]
        xbf_b = [Buf() for _ in xbf]

        def ln_stats_and_scale(xs_ap, dump_ap, mdst, rdst, c0, deps, out_ap, out_ev_needed=True):
            W = xs_ap.shape[-1]
            e1 = sch.op("dve", lambda: nc.vector.reduce_sum(
                cols[:, c0:c0 + 1], xs_ap, axis=AX.X), deps=deps, inc=True)
            e2 = sch.op("dve", lambda: nc.vector.tensor_scalar_mul(
                mdst, cols[:, c0:c0 + 1], 1.0 / W), deps=[e1], inc=True)
            e3 = sch.op("dve", lambda: nc.vector.tensor_single_scalar(
                xs_ap, xs_ap, mdst, ALU.subtract), deps=[e2], inc=True)
            e4 = sch.op("act", lambda: nc.scalar.activation(
                dump_ap, xs_ap, AF.Square, accum_out=cols[:, c0 + 1:c0 + 2]),
                deps=[e3], inc=True)
            e5 = sch.op("act", lambda: nc.scalar.activation(
                cols[:, c0 + 2:c0 + 3], cols[:, c0 + 1:c0 + 2], AF.Sqrt,
                bias=EPS_LN, scale=1.0 / W), deps=[e4], inc=True)
            e6 = sch.op("dve", lambda: nc.vector.reciprocal(
                rdst, cols[:, c0 + 2:c0 + 3]), deps=[e5], inc=True)
            e7 = sch.op("dve", lambda: nc.vector.tensor_single_scalar(
                out_ap, xs_ap, rdst, ALU.mult), deps=[e6], inc=True)
            return e7

        xin_ready_by = {}
        for m in range(NT):
            slot = m % 2
            b = xt_b[slot]
            ld = sch.dma("sp", (lambda mm=m, ss=slot: lambda: nc.sync.dma_start(
                out=xt[ss][:], in_=x_hbm[mm * 128:(mm + 1) * 128, :]))(),
                deps=sch.wr(b), sem=f"d_x{slot}")
            sch.set_ready(b, ld)
            bb = xbf_b[slot]
            e7 = ln_stats_and_scale(xt[slot][:], xbf[slot][:],
                                    m1_sb[:, m:m + 1], r1_sb[:, m:m + 1],
                                    (m % 2) * 4,
                                    sch.rd(b) + sch.wr(bb), xbf[slot][:])
            sch.set_ready(bb, e7)
            sch.add_reader(b, e7)
            srcs = [xbf[slot][:, ci * 128:(ci + 1) * 128] for ci in range(16)]
            dsts = [xin_fm[:, ci * T + m * 128: ci * T + (m + 1) * 128] for ci in range(16)]
            evs = transpose_tiles(srcs, dsts, sch.rd(bb))
            for e in evs:
                sch.add_reader(bb, e)
            if m == 7 or m == 15:
                xin_ready_by[m // 8] = [Ev(("s_act", sch.cnt["s_act"])),
                                        Ev(("s_dve", sch.cnt["s_dve"]))]

        memnorm = sb("memnorm", [128, HQ * NFC * MNP], BF16, at_ctx)
        norm_tt = sb("norm_tt", [128, NFC * 16], F32, at_ctx)
        norm16 = sb("norm16", [16, FD], F32, at_ctx)
        wkg = sb("wkg", [128, 16 * 128], BF16, at_ctx)
        wvg = sb("wvg", [128, 16 * 128], BF16, at_ctx)
        wmbg = [sb(f"wmbg{i}", [128, 16 * 128], BF16, at_ctx) for i in range(2)]
        r_sb = sb("r_sb", [128, 8 * 256], F32, at_ctx)
        v_sb = sb("v_sb", [128, 8 * 128], F32, at_ctx)
        mkraw = sb("mkraw", [128, 8 * FD], BF16, at_ctx)
        mkfm = sb("mkfm", [128, NFC * 1024], BF16, at_ctx)
        mb_sb = [sb(f"mb_sb{i}", [128, 8 * 128], F32, at_ctx) for i in range(2)]
        wmv_sb = [sb(f"wmv{i}", [128, 8 * 128], BF16, at_ctx) for i in range(2)]
        dsb = [sb(f"dsb{i}", [128, FD], F32, at_ctx) for i in range(1)]
        dnsb = sb("dnsb", [4, FD], F32, at_ctx)
        coefp = sb("coefp", [128, 8 * 4], BF16, at_ctx)
        kc = sb("kc", [128, 64], F32, at_ctx)
        sqdump = sb("sqdump", [128, FD], BF16, at_ctx)
        t2_sb = sb("t2_sb", [128, 2 * 128], F32, at_ctx)
        g_sb = [sb(f"g_sb{i}", [128, 8 * 128], F32, at_ctx) for i in range(2)]
        xc_sb = sb("xc_sb", [128, 2 * 128], F32, at_ctx)
        aout_sb = [sb(f"aout_sb{i}", [128, 8 * 128], F32, at_ctx) for i in range(1)]
        mstg = dsb[0]
        dstg = sb("dstg", [128, FD], F32, at_ctx)
        bslot = [sb(f"bslot{i}", [1, 128], BF16, at_ctx) for i in range(2)]

        wkg_b, wvg_b = Buf(), Buf()
        wmbg_b = [Buf(), Buf()]
        r_b, v_b = Buf(), Buf()
        mkraw_b, mkfm_b = Buf(), Buf()
        mb_b = [Buf(), Buf()]
        wmv_b = [Buf(), Buf()]
        dsb_b = [Buf()]
        dnsb_b = Buf()
        t2_b = [Buf(), Buf()]
        g_b = [Buf(), Buf()]
        xc_b = [Buf(), Buf()]
        aout_b = [Buf()]
        mstg_b, dstg_b = dsb_b[0], Buf()
        bslot_b = [Buf(), Buf()]
        memnorm_b = [Buf(), Buf()]
        ntt_b = Buf()
        coefp_b = Buf()
        xt2_b = xt_b

        def build_memnorm():
            mev = None
            for h in range(HQ):
                dst = memnorm[:, h * NFC * MNP:(h + 1) * NFC * MNP]
                dst = dst.rearrange("p (f c) -> p f c", c=MNP)[:, :, 0:128]
                src = mem_hbm[h].rearrange("f p d -> p f d")
                mev = sch.dma("pool", (lambda dst=dst, src=src: lambda:
                              nc.gpsimd.dma_start(out=dst, in_=src))(), sem="d_mem")
            nev = sch.dma("sp", lambda: nc.sync.dma_start(out=norm16[:], in_=normf_hbm),
                          sem="d_mem2")
            srcs = [norm16[:, fc * 128:(fc + 1) * 128] for fc in range(NFC)]
            tev = None
            for fc in range(NFC):
                tev = sch.op("pe", (lambda fc=fc: lambda: nc.tensor.transpose(
                    p512[5][:, fc * 16:(fc + 1) * 16], norm16[:, fc * 128:(fc + 1) * 128],
                    identf[0:16, 0:16]))(),
                    deps=([nev] + sch.wr(pb[5])) if fc == 0 else (), inc=(fc == NFC - 1))
            sch.set_ready(pb[5], tev)
            cev = sch.op("dve", lambda: nc.vector.tensor_copy(norm_tt[:], p512[5][:, 0:96]),
                         deps=sch.rd(pb[5]), inc=True)
            sch.add_reader(pb[5], cev)
            sch.set_ready(ntt_b, cev)
            lastc = None
            for fc in range(NFC):
                dst = memnorm[:, 0:HQ * NFC * MNP].rearrange(
                    "p (h c) -> p h c", c=NFC * MNP)[:, :, fc * MNP + 128: fc * MNP + 129]
                lastc = sch.op("dve", (lambda fc=fc, dst=dst: lambda: nc.vector.tensor_copy(
                    dst, norm_tt[:, fc * 16:(fc + 1) * 16].rearrange("p (h o) -> p h o", o=1)))(),
                    deps=[cev] if fc == 0 else (), inc=(fc == NFC - 1))
            return [mev, lastc]

        memnorm_ready = build_memnorm()

        def proj8(th, rhs_fn, bias_off, evac_fn, wbuf, extra_deps=()):
            bs = th % 2
            bev = sch.dma("pool", (lambda bs=bs, bias_off=bias_off: lambda:
                          nc.gpsimd.dma_start(
                              out=bslot[bs][:],
                              in_=bias_hbm[0:1, bias_off:bias_off + 128]))(),
                          deps=sch.wr(bslot_b[bs]), sem=f"d_b{bs}")
            sch.set_ready(bslot_b[bs], bev)
            evs = []
            for half4 in range(2):
                bank, bb_ = p512[half4], pb[half4]
                deps = sch.wr(bb_) + xin_ready_by[th] + sch.rd(wbuf) + [bev] + list(extra_deps)
                mev = None
                for j in range(4):
                    m = th * 8 + half4 * 4 + j
                    for ci in range(16):
                        mev = sch.op("pe", (lambda j=j, m=m, ci=ci, bank=bank: lambda:
                                     nc.tensor.matmul(
                                         bank[:, j * 128:(j + 1) * 128],
                                         xin_fm[:, ci * T + m * 128: ci * T + (m + 1) * 128],
                                         rhs_fn(ci), start=(ci == 0), stop=False))(),
                                     deps=deps if (j == 0 and ci == 0) else ())
                    mev = sch.op("pe", (lambda j=j, bank=bank, bs=bs: lambda:
                                 nc.tensor.matmul(
                                     bank[:, j * 128:(j + 1) * 128],
                                     ones_row[0:1, :], bslot[bs][:],
                                     start=False, stop=True))(),
                                 inc=(j == 3))
                sch.set_ready(bb_, mev)
                sch.add_reader(wbuf, mev)
                sch.add_reader(bslot_b[bs], mev)
                eev = evac_fn(bank, half4, sch.rd(bb_))
                sch.add_reader(bb_, eev)
                evs.append(eev)
            return evs

        def evac_relu_pair(dst_sb, dst_buf):
            def f(bank, half4, deps):
                base = half4 * 4 * 256
                view = dst_sb[:, base:base + 4 * 256].rearrange("p (m c) -> p m c", c=256)
                dst_p = view[:, :, 0:128]
                dst_n = view[:, :, 128:256]
                src = bank[:].rearrange("p (m c) -> p m c", c=128)
                sch.op("act", (lambda dst_p=dst_p, src=src: lambda: nc.scalar.activation(
                    dst_p, src, AF.Relu))(), deps=deps + sch.wr(dst_buf), inc=True)
                e2 = sch.op("act", (lambda dst_n=dst_n, src=src: lambda: nc.scalar.activation(
                    dst_n, src, AF.Relu, scale=-1.0))(), inc=True)
                return e2
            return f

        def evac_copy(dst_sb, dst_buf, func=None):
            def f(bank, half4, deps):
                dst = dst_sb[:, half4 * 512:(half4 + 1) * 512]
                if func is None:
                    return sch.op("act", (lambda dst=dst, bank=bank: lambda:
                                  nc.scalar.copy(dst, bank[:]))(),
                                  deps=deps + sch.wr(dst_buf), inc=True)
                return sch.op("act", (lambda dst=dst, bank=bank: lambda:
                              nc.scalar.activation(dst, bank[:], func))(),
                              deps=deps + sch.wr(dst_buf), inc=True)
            return f

        def dpfp8(kevs, extra_wr=()):
            prod_evs = []
            for mm in range(8):
                pe_ = None
                deps = (sch.wr(mkraw_b) + list(kevs)) if mm == 0 else []
                for j in range(1, NU + 1):
                    base = mm * FD + (j - 1) * 256
                    rb = mm * 256
                    sch.op("dve", (lambda base=base, rb=rb, j=j: lambda:
                           nc.vector.tensor_mul(
                               mkraw[:, base + j: base + 256],
                               r_sb[:, rb + j: rb + 256],
                               r_sb[:, rb: rb + 256 - j]))(),
                           deps=deps if j == 1 else ())
                    pe_ = sch.op("dve", (lambda base=base, rb=rb, j=j: lambda:
                                 nc.vector.tensor_mul(
                                     mkraw[:, base: base + j],
                                     r_sb[:, rb: rb + j],
                                     r_sb[:, rb + 256 - j: rb + 256]))(),
                                 inc=(j == NU))
                prod_evs.append(pe_)
            sch.set_ready(mkraw_b, prod_evs[-1])
            for e in prod_evs:
                sch.add_reader(r_b, e)
            sev = None
            for mm in range(8):
                sev = sch.op("act", (lambda mm=mm: lambda: nc.scalar.activation(
                    sqdump[:], mkraw[:, mm * FD:(mm + 1) * FD], AF.Square,
                    accum_out=kc[:, mm:mm + 1]))(),
                    deps=[prod_evs[mm]], inc=(mm == 7))
            nev = sch.op("act", lambda: nc.scalar.activation(
                kc[:, 8:16], kc[:, 0:8], AF.Sqrt), deps=[sev], inc=True)
            rnev = sch.op("dve", lambda: nc.vector.reciprocal(kc[:, 16:24], kc[:, 8:16]),
                          deps=[nev], inc=True)
            rqev = sch.op("dve", lambda: nc.vector.reciprocal(kc[:, 24:32], kc[:, 0:8]),
                          deps=[sev], inc=True)
            srcs, dsts = [], []
            for mm in range(8):
                for fc in range(NFC):
                    srcs.append(mkraw[:, mm * FD + fc * 128: mm * FD + (fc + 1) * 128])
                    dsts.append(mkfm[:, fc * 1024 + mm * 128: fc * 1024 + (mm + 1) * 128])
            tevs = transpose_tiles(srcs, dsts,
                                   [prod_evs[-1]] + sch.wr(mkfm_b) + list(extra_wr))
            for e in tevs:
                sch.add_reader(mkraw_b, e)
            fm_ready = [Ev(("s_act", sch.cnt["s_act"])), Ev(("s_dve", sch.cnt["s_dve"]))]
            return fm_ready, [nev, rnev, rqev]

        def numden(qh, mn_deps, fm_ready):
            bank_evs = []
            dcp = None
            for pair in range(4):
                bank, bb_ = p512[2 + pair], pb[2 + pair]
                deps = sch.wr(bb_) + fm_ready + list(mn_deps)
                mev = None
                for j in range(2):
                    mm = pair * 2 + j
                    for fc in range(NFC):
                        mev = sch.op("pe", (lambda j=j, mm=mm, fc=fc, bank=bank, qh=qh:
                                     lambda: nc.tensor.matmul(
                                         bank[:, j * MNP:j * MNP + MNP],
                                         mkfm[:, fc * 1024 + mm * 128: fc * 1024 + (mm + 1) * 128],
                                         memnorm[:, (qh * NFC + fc) * MNP:(qh * NFC + fc + 1) * MNP],
                                         start=(fc == 0), stop=(fc == NFC - 1)))(),
                                     deps=deps if (j == 0 and fc == 0) else (),
                                     inc=(j == 1 and fc == NFC - 1))
                sch.set_ready(bb_, mev)
                sch.add_reader(memnorm_b[qh // 8], mev)
                densrc = bank[:, 0:2 * MNP].rearrange("p (j c) -> p j c", c=MNP)[:, :, 128:129]
                densrc = densrc.rearrange("p j c -> p (j c)")
                dcp = sch.op("dve", (lambda densrc=densrc, pair=pair: lambda:
                             nc.vector.tensor_copy(
                                 kc[:, 32 + 2 * pair:34 + 2 * pair], densrc))(),
                             deps=sch.rd(bb_), inc=True)
                sch.add_reader(bb_, dcp)
                bank_evs.append(mev)
            e1 = sch.op("dve", lambda: nc.vector.scalar_tensor_tensor(
                kc[:, 40:48], kc[:, 8:16], -EPS_AT, kc[:, 32:40],
                ALU.mult, ALU.subtract), deps=[dcp], inc=True)
            e2 = sch.op("dve", lambda: nc.vector.reciprocal(kc[:, 48:56], kc[:, 40:48]),
                        deps=[e1], inc=True)
            return bank_evs, e2

        delta_dma_evs = [[], []]

        def kside_group(g):
            wk_ev = sch.dma("pool", (lambda g=g: lambda: nc.gpsimd.dma_start(
                out=wkg[:].rearrange("p (c o) -> p c o", o=128),
                in_=wk_hbm[:, :, g * 128:(g + 1) * 128]
                .rearrange("c p o -> p c o")))(),
                deps=sch.wr(wkg_b), sem="d_wk")
            sch.set_ready(wkg_b, wk_ev)
            wv_ev = sch.dma("pool", (lambda g=g: lambda: nc.gpsimd.dma_start(
                out=wvg[:].rearrange("p (c o) -> p c o", o=128),
                in_=wv_hbm[:, :, g * 128:(g + 1) * 128]
                .rearrange("c p o -> p c o")))(),
                deps=sch.wr(wvg_b), sem="d_wv")
            sch.set_ready(wvg_b, wv_ev)
            for th in range(2):
                kevs = proj8(th, lambda ci: wkg[:, ci * 128:(ci + 1) * 128],
                             BK0 + g * 128, evac_relu_pair(r_sb, r_b), wkg_b)
                sch.set_ready(r_b, kevs[-1])
                vevs = proj8(th, lambda ci: wvg[:, ci * 128:(ci + 1) * 128],
                             BV0 + g * 128, evac_copy(v_sb, v_b), wvg_b)
                sch.set_ready(v_b, vevs[-1])
                fm_ready, (nev, rnev, rqev) = dpfp8(kevs)
                for q4 in range(4):
                    qh = g * 4 + q4
                    ws = q4 % 2
                    wmb_ev = sch.dma("pool", (lambda qh=qh, ws=ws: lambda:
                                     nc.gpsimd.dma_start(
                                         out=wmbg[ws][:].rearrange("p (c o) -> p c o", o=128),
                                         in_=wmb_hbm[:, :, qh * 128:(qh + 1) * 128]
                                         .rearrange("c p o -> p c o")))(),
                                     deps=sch.wr(wmbg_b[ws]), sem=f"d_wmb{ws}")
                    sch.set_ready(wmbg_b[ws], wmb_ev)
                    mslot = q4 % 2
                    mevs = proj8(th, (lambda ws=ws: lambda ci: wmbg[ws][:, ci * 128:(ci + 1) * 128])(),
                                 BMB0 + qh * 128,
                                 evac_copy(mb_sb[mslot], mb_b[mslot], AF.Sigmoid),
                                 wmbg_b[ws])
                    sch.set_ready(mb_b[mslot], mevs[-1])
                    bank_evs, rden_ev = numden(qh, memnorm_ready, fm_ready)
                    wslot_w = q4 % 2
                    wmv_evs = []
                    for mm in range(8):
                        pair = mm // 2
                        j = mm % 2
                        bank, bb_ = p512[2 + pair], pb[2 + pair]
                        num = bank[:, j * MNP:j * MNP + 128]
                        t2slot = mm % 2
                        e3 = sch.op("dve", (lambda num=num, mm=mm, t2slot=t2slot: lambda:
                                    nc.vector.scalar_tensor_tensor(
                                        t2_sb[:, t2slot * 128:(t2slot + 1) * 128],
                                        num, kc[:, 48 + mm:49 + mm],
                                        v_sb[:, mm * 128:(mm + 1) * 128],
                                        ALU.mult, ALU.add))(),
                                    deps=[rden_ev] + sch.rd(v_b, mb_b[mslot])
                                    + sch.wr(t2_b[t2slot]), inc=True)
                        sch.set_ready(t2_b[t2slot], e3)
                        e4 = sch.op("dve", (lambda mm=mm, t2slot=t2slot, ws=wslot_w: lambda:
                                    nc.vector.scalar_tensor_tensor(
                                        wmv_sb[ws][:, mm * 128:(mm + 1) * 128],
                                        t2_sb[:, t2slot * 128:(t2slot + 1) * 128],
                                        kc[:, 16 + mm:17 + mm],
                                        mb_sb[mslot][:, mm * 128:(mm + 1) * 128],
                                        ALU.mult, ALU.mult))(),
                                    deps=[e3] + sch.wr(wmv_b[wslot_w]) if mm == 0 else [e3],
                                    inc=True)
                        sch.add_reader(t2_b[t2slot], e4)
                        wmv_evs.append(e4)
                        sch.add_reader(bb_, e4)
                    sch.set_ready(wmv_b[wslot_w], wmv_evs[-1])
                    c1 = sch.op("dve", lambda: nc.vector.tensor_mul(
                        kc[:, 56:64], kc[:, 40:48], kc[:, 8:16]),
                        deps=[rden_ev], inc=True)
                    c2 = sch.op("dve", lambda: nc.vector.tensor_mul(
                        kc[:, 56:64], kc[:, 56:64], kc[:, 24:32]), deps=[c1], inc=True)
                    c3 = sch.op("dve", lambda: nc.vector.tensor_scalar(
                        kc[:, 56:64], kc[:, 56:64], 1.0, 0.0, ALU.add, ALU.max),
                        deps=[c2], inc=True)
                    c4 = sch.op("dve", lambda: nc.vector.tensor_scalar_min(
                        kc[:, 56:64], kc[:, 56:64], 1.0), deps=[c3], inc=True)
                    dstc = coefp[:].rearrange("p (m c) -> p m c", c=4)[:, :, q4:q4 + 1]
                    c5 = sch.op("dve", (lambda dstc=dstc: lambda: nc.vector.tensor_mul(
                        dstc, kc[:, 56:64].rearrange("p (m o) -> p m o", o=1),
                        kc[:, 16:24].rearrange("p (m o) -> p m o", o=1)))(),
                        deps=[c4] + sch.wr(coefp_b) if q4 == 0 else [c4], inc=True)
                    sch.add_reader(coefp_b, c5)
                    if q4 == 3:
                        sch.set_ready(coefp_b, c5)
                    dev_ = None
                    dbanks = [pb[2], pb[3]]
                    ddeps = sch.wr(*dbanks) + [wmv_evs[0]] + sch.rd(mkraw_b)
                    for mm in range(8):
                        for fc in range(NFC):
                            bank = p512[2] if fc < 4 else p512[3]
                            off = fc * 128 if fc < 4 else (fc - 4) * 128
                            if mm == 0 and fc == 0:
                                mdeps = ddeps
                            elif fc == 0:
                                mdeps = [wmv_evs[mm]]
                            else:
                                mdeps = ()
                            dev_ = sch.op("pe", (lambda mm=mm, fc=fc, bank=bank, off=off,
                                          ws=wslot_w: lambda: nc.tensor.matmul(
                                              bank[:, off:off + 128],
                                              mkraw[:, mm * FD + fc * 128: mm * FD + (fc + 1) * 128],
                                              wmv_sb[ws][:, mm * 128:(mm + 1) * 128],
                                              start=(mm == 0), stop=(mm == 7)))(),
                                          deps=mdeps,
                                          inc=(mm == 7 and fc == NFC - 1))
                    sch.set_ready(pb[2], dev_)
                    sch.set_ready(pb[3], dev_)
                    sch.add_reader(wmv_b[wslot_w], dev_)
                    dslot = 0
                    ee1 = sch.op("dve", (lambda dslot=dslot: lambda: nc.vector.tensor_copy(
                        dsb[dslot][:, 0:512], p512[2][:]))(),
                        deps=[dev_] + sch.wr(dsb_b[dslot]), inc=True)
                    ee2 = sch.op("dve", (lambda dslot=dslot: lambda: nc.vector.tensor_copy(
                        dsb[dslot][:, 512:768], p512[3][:, 0:256]))(), inc=True)
                    sch.add_reader(pb[2], ee2)
                    sch.add_reader(pb[3], ee2)
                    sch.set_ready(dsb_b[dslot], ee2)
                    buf_idx = qh // 8
                    hh = qh % 8
                    dst = dAB_in[buf_idx][hh:hh + 1, 0:FD * HD].rearrange(
                        "o (f p d) -> p (o f) d", f=NFC, p=128, d=128)
                    dde = sch.dma("pool", (lambda dst=dst, dslot=dslot, th=th: lambda:
                                  nc.gpsimd.dma_start(
                                      out=dst,
                                      in_=dsb[dslot][:].rearrange("p (f d) -> p f d", d=128),
                                      accum_op=(ALU.add if th == 1 else ALU.bypass)))(),
                                  deps=sch.rd(dsb_b[dslot]), sem=f"d_dd{dslot}")
                    sch.add_reader(dsb_b[dslot], dde)
                    delta_dma_evs[buf_idx].append(dde)
                dn_evs = []
                for part in range(2):
                    n0, nw = (0, 512) if part == 0 else (512, 256)
                    dnv = None
                    for mm in range(8):
                        dnv = sch.op("pe", (lambda mm=mm, n0=n0, nw=nw: lambda:
                                     nc.tensor.matmul(
                                         p512[4][0:4, 0:nw],
                                         coefp[:, mm * 4:(mm + 1) * 4],
                                         mkraw[:, mm * FD + n0: mm * FD + n0 + nw],
                                         start=(mm == 0), stop=(mm == 7)))(),
                                     deps=(sch.wr(pb[4]) + sch.rd(coefp_b, mkraw_b))
                                     if mm == 0 else (),
                                     inc=(mm == 7))
                    sch.set_ready(pb[4], dnv)
                    ev = sch.op("dve", (lambda n0=n0, nw=nw: lambda: nc.vector.tensor_copy(
                        dnsb[0:4, n0:n0 + nw], p512[4][0:4, 0:nw]))(),
                        deps=sch.rd(pb[4]) + (sch.wr(dnsb_b) if part == 0 else []),
                        inc=True)
                    sch.add_reader(pb[4], ev)
                    dn_evs.append(ev)
                sch.set_ready(dnsb_b, dn_evs[-1])
                buf_idx = g // 2
                row0 = (g % 2) * 4
                dnd = sch.dma("pool", (lambda buf_idx=buf_idx, row0=row0, th=th: lambda:
                              nc.gpsimd.dma_start(
                                  out=dAB_in[buf_idx][row0:row0 + 4, FD * HD:FD * HD + FD],
                                  in_=dnsb[0:4, :],
                                  accum_op=(ALU.add if th == 1 else ALU.bypass)))(),
                              deps=sch.rd(dnsb_b), sem="d_dn")
                sch.add_reader(dnsb_b, dnd)
                delta_dma_evs[buf_idx].append(dnd)

        ar_evs = [None, None]
        newmem_ready = [None, None]

        def issue_collective(buf_idx):
            ev = sch.op("pool", (lambda b=buf_idx: lambda: nc.gpsimd.collective_compute(
                "AllReduce", ALU.add, replica_groups=[list(range(8))],
                ins=[dAB_in[b].opt()], outs=[dAB_out[b].opt()]))(),
                deps=delta_dma_evs[buf_idx], inc=True, sem="c_ar")
            ar_evs[buf_idx] = ev

        def update_memnorm(buf_idx):
            half = buf_idx
            evs = []
            for hh in range(8):
                h = half * 8 + hh
                mev = sch.dma("sp", (lambda h=h: lambda: nc.sync.dma_start(
                    out=mstg[:].rearrange("p (f d) -> p f d", d=128),
                    in_=mem_hbm[h].rearrange("f p d -> p f d")))(),
                    deps=sch.wr(mstg_b), sem="d_ms")
                sch.set_ready(mstg_b, mev)
                dev_ = sch.dma("sp", (lambda hh=hh, b=buf_idx: lambda: nc.sync.dma_start(
                    out=dstg[:].rearrange("p (f d) -> p f d", d=128),
                    in_=dAB_out[b][hh:hh + 1, 0:FD * HD]
                    .rearrange("o (f p d) -> p (o f) d", f=NFC, p=128, d=128)))(),
                    deps=[ar_evs[buf_idx]] + sch.wr(dstg_b), sem="d_ds")
                sch.set_ready(dstg_b, dev_)
                dst = memnorm[:, h * NFC * MNP:(h + 1) * NFC * MNP]
                dst = dst.rearrange("p (f c) -> p f c", c=MNP)[:, :, 0:128]
                aev = sch.op("dve", (lambda dst=dst: lambda: nc.vector.tensor_add(
                    dst, mstg[:].rearrange("p (f d) -> p f d", d=128),
                    dstg[:].rearrange("p (f d) -> p f d", d=128)))(),
                    deps=[mev, dev_] + (memnorm_b[half].readers if hh == 0 else []),
                    inc=True)
                sch.add_reader(mstg_b, aev)
                sch.add_reader(dstg_b, aev)
                evs.append(aev)
            ndev = sch.dma("sp", (lambda b=buf_idx: lambda: nc.sync.dma_start(
                out=norm16[0:8, :], in_=dAB_out[b][0:8, FD * HD:FD * HD + FD]))(),
                deps=[ar_evs[buf_idx]] + sch.wr(ntt_b), sem="d_dn2")
            for fc in range(NFC):
                tev = sch.op("pe", (lambda fc=fc: lambda: nc.tensor.transpose(
                    p512[5][:, 0:8], norm16[0:8, fc * 128:(fc + 1) * 128],
                    identf[0:8, 0:8]))(),
                    deps=[ndev] + sch.wr(pb[5]), inc=True)
                sch.set_ready(pb[5], tev)
                dst = memnorm[:, half * 8 * NFC * MNP:(half * 8 + 8) * NFC * MNP]
                dst = dst.rearrange("p (h c) -> p h c", c=NFC * MNP)[
                    :, :, fc * MNP + 128: fc * MNP + 129]
                ev = sch.op("dve", (lambda dst=dst, fc=fc, half=half: lambda:
                            nc.vector.tensor_add(
                                dst, norm_tt[:, fc * 16 + half * 8: fc * 16 + half * 8 + 8]
                                .rearrange("p (h o) -> p h o", o=1),
                                p512[5][:, 0:8].rearrange("p (h o) -> p h o", o=1)))(),
                    deps=sch.rd(pb[5]), inc=True)
                sch.add_reader(pb[5], ev)
                evs.append(ev)
            nmev = [Ev(("s_dve", sch.cnt["s_dve"]))]
            memnorm_b[half].ready = None
            memnorm_b[half].readers = []
            newmem_ready[half] = evs[-1:]
            newmem_ready[half] = nmev

        kside_group(0)
        kside_group(1)
        issue_collective(0)
        kside_group(2)
        kside_group(3)
        issue_collective(1)
        update_memnorm(0)

        def qside_head(qh):
            half = qh // 8
            if qh == 8:
                update_memnorm(1)
            ws = qh % 2
            wq_ev = sch.dma("pool", (lambda qh=qh, ws=ws: lambda: nc.gpsimd.dma_start(
                out=(wkg if ws == 0 else wvg)[:].rearrange("p (c o) -> p c o", o=128),
                in_=wq_hbm[:, :, qh * 128:(qh + 1) * 128].rearrange("c p o -> p c o")))(),
                deps=sch.wr(wkg_b if ws == 0 else wvg_b), sem=f"d_wq{ws}")
            wq_buf = wkg_b if ws == 0 else wvg_b
            wq_t = wkg if ws == 0 else wvg
            sch.set_ready(wq_buf, wq_ev)
            wg_ev = sch.dma("pool", (lambda qh=qh, ws=ws: lambda: nc.gpsimd.dma_start(
                out=wmbg[ws][:].rearrange("p (c o) -> p c o", o=128),
                in_=wg_hbm[:, :, qh * 128:(qh + 1) * 128].rearrange("c p o -> p c o")))(),
                deps=sch.wr(wmbg_b[ws]), sem=f"d_wg{ws}")
            sch.set_ready(wmbg_b[ws], wg_ev)
            xs = qh % 2
            xe1 = sch.dma("sp", (lambda qh=qh, xs=xs: lambda: nc.sync.dma_start(
                out=xt[xs][:, 0:1024].rearrange("p (m c) -> p m c", c=128),
                in_=x_hbm[0:1024, qh * 128:(qh + 1) * 128]
                .rearrange("(m p) c -> p m c", p=128)))(),
                deps=sch.wr(xt2_b[xs]), sem=f"d_xc{xs}")
            xe2 = sch.dma("sp", (lambda qh=qh, xs=xs: lambda: nc.sync.dma_start(
                out=xt[xs][:, 1024:2048].rearrange("p (m c) -> p m c", c=128),
                in_=x_hbm[1024:2048, qh * 128:(qh + 1) * 128]
                .rearrange("(m p) c -> p m c", p=128)))(), sem=f"d_xc{xs}")
            sch.set_ready(xt2_b[xs], xe2)
            xcol_ready = [xe1, xe2]
            for th in range(2):
                qevs = proj8(th, (lambda w=wq_t: lambda ci: w[:, ci * 128:(ci + 1) * 128])(),
                             BQ0 + qh * 128, evac_relu_pair(r_sb, r_b), wq_buf)
                sch.set_ready(r_b, qevs[-1])
                fm_ready, (nev, rnev, rqev) = dpfp8(qevs)
                gevs = proj8(th, (lambda ws=ws: lambda ci: wmbg[ws][:, ci * 128:(ci + 1) * 128])(),
                             BG0 + qh * 128, evac_copy(g_sb[th], g_b[th], AF.Sigmoid),
                             wmbg_b[ws])
                sch.set_ready(g_b[th], gevs[-1])
                bank_evs, rden_ev = numden(qh, newmem_ready[half], fm_ready)
                a_evs = []
                for mm in range(8):
                    m = th * 8 + mm
                    pair = mm // 2
                    j = mm % 2
                    bank, bb_ = p512[2 + pair], pb[2 + pair]
                    num = bank[:, j * MNP:j * MNP + 128]
                    xcslot = mm % 2
                    xce = sch.op("dve", (lambda m=m, xs=xs, xcslot=xcslot: lambda:
                                 nc.vector.tensor_scalar(
                                     xc_sb[:, xcslot * 128:(xcslot + 1) * 128],
                                     xt[xs][:, m * 128:(m + 1) * 128],
                                     m1_sb[:, m:m + 1], r1_sb[:, m:m + 1],
                                     ALU.subtract, ALU.mult))(),
                                 deps=xcol_ready + sch.wr(xc_b[xcslot]), inc=True)
                    sch.set_ready(xc_b[xcslot], xce)
                    sch.add_reader(xt2_b[xs], xce)
                    t2slot = mm % 2
                    e3 = sch.op("dve", (lambda num=num, mm=mm, th=th, t2slot=t2slot: lambda:
                                nc.vector.scalar_tensor_tensor(
                                    t2_sb[:, t2slot * 128:(t2slot + 1) * 128],
                                    num, kc[:, 48 + mm:49 + mm],
                                    g_sb[th][:, mm * 128:(mm + 1) * 128],
                                    ALU.mult, ALU.mult))(),
                                deps=[rden_ev] + sch.rd(g_b[th]) + sch.wr(t2_b[t2slot]),
                                inc=True)
                    sch.set_ready(t2_b[t2slot], e3)
                    sch.add_reader(bb_, e3)
                    aslot = 0
                    e4 = sch.op("dve", (lambda mm=mm, t2slot=t2slot, xcslot=xcslot,
                                aslot=aslot: lambda: nc.vector.scalar_tensor_tensor(
                                    aout_sb[aslot][:, mm * 128:(mm + 1) * 128],
                                    t2_sb[:, t2slot * 128:(t2slot + 1) * 128], -1.0,
                                    xc_sb[:, xcslot * 128:(xcslot + 1) * 128],
                                    ALU.mult, ALU.add,
                                    accum_out=cols[:, 8:9]))(),
                                deps=([e3, xce] + sch.wr(aout_b[0])) if mm == 0
                                else [e3, xce], inc=True)
                    sch.add_reader(t2_b[t2slot], e4)
                    sch.add_reader(xc_b[xcslot], e4)
                    e5 = sch.op("dve", (lambda m=m: lambda: nc.vector.tensor_add(
                        s2_sb[:, m:m + 1], s2_sb[:, m:m + 1], cols[:, 8:9]))(),
                        deps=[e4], inc=True)
                    e6 = sch.op("act", (lambda mm=mm, aslot=aslot: lambda:
                                nc.scalar.activation(
                                    sqdump[:, 0:128],
                                    aout_sb[aslot][:, mm * 128:(mm + 1) * 128],
                                    AF.Square, accum_out=cols[:, 9:10]))(),
                                deps=[e4], inc=True)
                    e7 = sch.op("dve", (lambda m=m: lambda: nc.vector.tensor_add(
                        q2_sb[:, m:m + 1], q2_sb[:, m:m + 1], cols[:, 9:10]))(),
                        deps=[e6], inc=True)
                    a_evs.append(e4)
                    last_sq = e6
                sch.set_ready(aout_b[0], a_evs[-1])
                sch.add_reader(aout_b[0], last_sq)
                dst = aout_dram[th * 1024:(th + 1) * 1024, qh * 128:(qh + 1) * 128]
                dst = dst.rearrange("(m p) c -> p m c", p=128)
                ade = sch.dma("sp", (lambda dst=dst, th=th: lambda: nc.sync.dma_start(
                    out=dst, in_=aout_sb[0][:].rearrange("p (m c) -> p m c", c=128)))(),
                    deps=sch.rd(aout_b[0]), sem=f"d_ao{th}")
                sch.add_reader(aout_b[0], ade)

        for qh in range(HQ):
            qside_head(qh)

        f1 = sch.op("dve", lambda: nc.vector.tensor_scalar_mul(
            m2_sb[:], s2_sb[:], 1.0 / DM), deps=sch.snapshot(), inc=True)
        f2 = sch.op("dve", lambda: nc.vector.tensor_scalar_mul(
            q2_sb[:], q2_sb[:], 1.0 / DM), deps=[f1], inc=True)
        f3 = sch.op("dve", lambda: nc.vector.tensor_mul(
            s2_sb[:], m2_sb[:], m2_sb[:]), deps=[f2], inc=True)
        f4 = sch.op("dve", lambda: nc.vector.tensor_sub(
            q2_sb[:], q2_sb[:], s2_sb[:]), deps=[f3], inc=True)
        f5 = sch.op("act", lambda: nc.scalar.activation(
            q2_sb[:], q2_sb[:], AF.Sqrt, bias=EPS_LN), deps=[f4], inc=True)
        f6 = sch.op("dve", lambda: nc.vector.reciprocal(r2_sb[:], q2_sb[:]),
                    deps=[f5], inc=True)
        ln2_ready = f6

        if "dAin" in taps:
            sch.dma("pool", lambda: nc.gpsimd.dma_start(out=taps["dAin"], in_=dAB_in[0]),
                    deps=delta_dma_evs[0], sem="d_tap")
        if "dAout" in taps:
            sch.dma("pool", lambda: nc.gpsimd.dma_start(out=taps["dAout"], in_=dAB_out[0]),
                    deps=[ar_evs[0]], sem="d_tap")

        sch.set_floor()
        at_ctx.close()

        ffn_ctx = ExitStack()
        x2h_fm = sb("x2h_fm", [128, 16 * 1024], BF16, ffn_ctx)
        x2h_b = Buf()
        p2_ctx = ExitStack()
        xrow = [sb(f"xrow{i}", [128, DM], F32, p2_ctx) for i in range(2)]
        xrow_b = [Buf(), Buf()]
        arow = [sb(f"arow{i}", [128, DM], F32, p2_ctx) for i in range(2)]
        arow_b = [Buf(), Buf()]
        x1t = [sb(f"x1t{i}", [128, DM], F32, p2_ctx) for i in range(2)]
        x1t_b = [Buf(), Buf()]
        x2bf = [sb(f"x2bf{i}", [128, DM], BF16, p2_ctx) for i in range(2)]
        x2bf_b = [Buf(), Buf()]

        x2h_evs = []
        for m in list(range(8, NT)) + list(range(8)):
            slot = m % 2
            xb = xrow_b[slot]
            ld = sch.dma("sp", (lambda mm=m, ss=slot: lambda: nc.sync.dma_start(
                out=xrow[ss][:], in_=x_hbm[mm * 128:(mm + 1) * 128, :]))(),
                deps=sch.wr(xb), sem=f"d_x{slot}")
            sch.set_ready(xb, ld)
            ab = arow_b[slot]
            ald = sch.dma("sp", (lambda mm=m, ss=slot: lambda: nc.sync.dma_start(
                out=arow[ss][:], in_=aout_dram[mm * 128:(mm + 1) * 128, :]))(),
                deps=sch.wr(ab), sem=f"d_ar{slot}")
            sch.set_ready(ab, ald)
            e1 = sch.op("dve", (lambda ss=slot, m=m: lambda: nc.vector.tensor_scalar(
                arow[ss][:], arow[ss][:], m2_sb[:, m:m + 1], r2_sb[:, m:m + 1],
                ALU.subtract, ALU.mult))(),
                deps=sch.rd(ab) + [ln2_ready], inc=True)
            xb1 = x1t_b[slot]
            e2 = sch.op("dve", (lambda ss=slot: lambda: nc.vector.tensor_add(
                x1t[ss][:], xrow[ss][:], arow[ss][:]))(),
                deps=[e1] + sch.rd(xb) + sch.wr(xb1), inc=True)
            sch.add_reader(xb, e2)
            sch.add_reader(ab, e2)
            sch.set_ready(xb1, e2)
            x1de = sch.dma("sp", (lambda mm=m, ss=slot: lambda: nc.sync.dma_start(
                out=x1_dram[mm * 128:(mm + 1) * 128, :], in_=x1t[ss][:]))(),
                deps=sch.rd(xb1), sem=f"d_x1{slot}")
            sch.add_reader(xb1, x1de)
            bb2 = x2bf_b[slot]
            e7 = ln_stats_and_scale(x1t[slot][:], arow[slot][:],
                                    m3_sb[:, m:m + 1], r3_sb[:, m:m + 1],
                                    (m % 2) * 4,
                                    [x1de] + sch.wr(bb2) + sch.rd(xb1), x2bf[slot][:])
            sch.set_ready(bb2, e7)
            sch.add_reader(xb1, e7)
            sch.add_reader(ab, e7)
            if m < 8:
                srcs = [x2bf[slot][:, ci * 128:(ci + 1) * 128] for ci in range(16)]
                dsts = [x2h_fm[:, ci * 1024 + m * 128: ci * 1024 + (m + 1) * 128]
                        for ci in range(16)]
                evs = transpose_tiles(srcs, dsts, sch.rd(bb2))
                for e in evs:
                    sch.add_reader(bb2, e)
                x2h_evs += evs[-1:]
                if m == 7:
                    x2h_ready = [Ev(("s_act", sch.cnt["s_act"])),
                                 Ev(("s_dve", sch.cnt["s_dve"]))]
            else:
                de = sch.dma("sp", (lambda mm=m, ss=slot: lambda: nc.sync.dma_start(
                    out=x2b_dram[mm - 8], in_=x2bf[ss][:]))(),
                    deps=sch.rd(bb2), sem=f"d_x2b{slot}")
                sch.add_reader(bb2, de)

        if "x1d" in taps:
            sch.dma("pool", lambda: nc.gpsimd.dma_start(out=taps["x1d"], in_=x1_dram),
                    deps=sch.snapshot(), sem="d_tap")
        if "aoutd" in taps:
            sch.dma("pool", lambda: nc.gpsimd.dma_start(out=taps["aoutd"], in_=aout_dram),
                    deps=sch.snapshot(), sem="d_tap")

        sch.set_floor()
        p2_ctx.close()

        acc = sb("acc", [128, 16 * 1024], F32, ffn_ctx)
        h_sb = sb("h_sb", [128, 11 * 1024], BF16, ffn_ctx)
        wblk = [sb(f"wblk{i}", [128, 2048], BF16, ffn_ctx) for i in range(4)]
        w2blk = [sb(f"w2blk{i}", [128, 11 * 128], BF16, ffn_ctx) for i in range(2)]
        sdwblk = w2blk
        gw_sb = sb("gw_sb", [128, 16 * NE], BF16, ffn_ctx)
        sgw_sb = sb("sgw_sb", [128, 16], BF16, ffn_ctx)
        lg_sb = sb("lg_sb", [128, 8 * NE], F32, ffn_ctx)
        wrk_sb = sb("wrk_sb", [128, 8 * NE], F32, ffn_ctx)
        coef_tm = sb("coef_tm", [128, 8 * NE], F32, ffn_ctx)
        coef_fm = sb("coef_fm", [16, 1024], F32, ffn_ctx)
        sg_row = sb("sg_row", [1, 1024], F32, ffn_ctx)
        ones_f32 = sb("ones_f32", [1, 128], F32, ffn_ctx)
        onehot_sb = sb("onehot", [16, 128], F32, ffn_ctx)
        cr_sb = sb("cr_sb", [128, 1024], F32, ffn_ctx)
        sgr_sb = sb("sgr_sb", [128, 1024], F32, ffn_ctx)
        gs_sb = sb("gs_sb", [128, 1024], BF16, ffn_ctx)
        us_sb = sb("us_sb", [128, 1024], BF16, ffn_ctx)
        acct = [sb(f"acct{i}", [128, DM], F32, ffn_ctx) for i in range(2)]
        x1f = [sb(f"x1f{i}", [128, DM], F32, ffn_ctx) for i in range(2)]
        xrb = [sb(f"xrb{i}", [128, DM], BF16, ffn_ctx) for i in range(2)]
        fdump = xrb[1]
        fcols = sb("fcols", [128, 16], F32, ffn_ctx)

        acc_b = Buf()
        h_b = Buf()
        wblk_b = [Buf() for _ in wblk]
        w2blk_b = [Buf() for _ in w2blk]
        sdwblk_b = w2blk_b
        gw_b, sgw_b = Buf(), Buf()
        lg_b, cfm_b = Buf(), Buf()
        sgrow_b, cr_b, sgr_b = Buf(), Buf(), Buf()
        gs_b, us_b = Buf(), Buf()
        acct_b = [Buf(), Buf()]
        x1f_b = [Buf(), Buf()]
        xrb_b = [Buf(), Buf()]

        sch.op("pool", lambda: nc.gpsimd.memset(ones_f32[:], 1.0))
        onehot_b = Buf()

        def set_onehot(e):
            mz = sch.op("pool", lambda: nc.gpsimd.memset(onehot_sb[:], 0.0),
                        deps=sch.wr(onehot_b), inc=True)
            ev = sch.op("pool", (lambda e=e: lambda: nc.gpsimd.affine_select(
                out=onehot_sb[:], in_=onehot_sb[:], compare_op=ALU.not_equal,
                fill=1.0, base=-e, pattern=[[0, 128]], channel_multiplier=1))(),
                deps=[mz], inc=True)
            sch.set_ready(onehot_b, ev)
        gwe = sch.dma("pool", lambda: nc.gpsimd.dma_start(
            out=gw_sb[:].rearrange("p (c o) -> p c o", o=NE),
            in_=gatew_hbm.rearrange("c p o -> p c o")), sem="d_gw")
        sch.set_ready(gw_b, gwe)
        sgwe = sch.dma("pool", lambda: nc.gpsimd.dma_start(
            out=sgw_sb[:], in_=sgatew_hbm), sem="d_gw2")
        sch.set_ready(sgw_b, sgwe)

        def x2_slice(ci, tg):
            return x2h_fm[:, ci * 1024 + tg * 512: ci * 1024 + (tg + 1) * 512]

        def router(hf, x2_ready):
            bank, bb_ = p512[0], pb[0]
            mev = None
            deps = sch.wr(bb_) + list(x2_ready) + sch.rd(gw_b)
            for m8 in range(8):
                for ci in range(16):
                    mev = sch.op("pe", (lambda m8=m8, ci=ci, bank=bank: lambda:
                                 nc.tensor.matmul(
                                     bank[:, m8 * 16:(m8 + 1) * 16],
                                     x2h_fm[:, ci * 1024 + m8 * 128: ci * 1024 + (m8 + 1) * 128],
                                     gw_sb[:, ci * 16:(ci + 1) * 16],
                                     start=(ci == 0), stop=(ci == 15)))(),
                                 deps=deps if (m8 == 0 and ci == 0) else (),
                                 inc=(m8 == 7 and ci == 15))
            sch.set_ready(bb_, mev)
            sch.add_reader(gw_b, mev)
            lev = sch.op("dve", lambda: nc.vector.tensor_copy(lg_sb[:], p512[0][:, 0:128]),
                         deps=sch.rd(bb_) + sch.wr(lg_b), inc=True)
            sch.add_reader(bb_, lev)
            sch.set_ready(lg_b, lev)
            cevs = []
            for m8 in range(8):
                s0, s1 = m8 * 16, (m8 + 1) * 16
                e1 = sch.op("dve", (lambda s0=s0, s1=s1: lambda: nc.vector.reduce_max(
                    fcols[:, 0:1], lg_sb[:, s0:s1], axis=AX.X, negate=True))(),
                    deps=[lev] if m8 == 0 else [], inc=True)
                e2 = sch.op("act", (lambda s0=s0, s1=s1: lambda: nc.scalar.activation(
                    wrk_sb[:, s0:s1], lg_sb[:, s0:s1], AF.Exp, bias=fcols[:, 0:1]))(),
                    deps=[e1], inc=True)
                e3 = sch.op("dve", (lambda s0=s0, s1=s1: lambda: nc.vector.reduce_sum(
                    fcols[:, 1:2], wrk_sb[:, s0:s1], axis=AX.X))(), deps=[e2], inc=True)
                e4 = sch.op("dve", lambda: nc.vector.reciprocal(
                    fcols[:, 2:3], fcols[:, 1:2]), deps=[e3], inc=True)
                e5 = sch.op("dve", (lambda s0=s0, s1=s1: lambda:
                            nc.vector.tensor_single_scalar(
                                wrk_sb[:, s0:s1], wrk_sb[:, s0:s1], fcols[:, 2:3],
                                ALU.mult))(), deps=[e4], inc=True)
                prev = sch.op("dve", (lambda s0=s0, s1=s1: lambda: nc.vector.tensor_copy(
                    coef_tm[:, s0:s1], wrk_sb[:, s0:s1]))(), deps=[e5], inc=True)
                for r_ in range(4):
                    mx = sch.op("dve", (lambda s0=s0, s1=s1: lambda: nc.vector.reduce_max(
                        fcols[:, 3:4], coef_tm[:, s0:s1], axis=AX.X))(),
                        deps=[prev], inc=True)
                    if r_ < 3:
                        mk_ = sch.op("dve", (lambda s0=s0, s1=s1: lambda:
                                     nc.vector.tensor_single_scalar(
                                         lg_sb[:, s0:s1], coef_tm[:, s0:s1],
                                         fcols[:, 3:4], ALU.is_equal))(),
                                     deps=[mx], inc=True)
                        prev = sch.op("dve", (lambda s0=s0, s1=s1: lambda:
                                      nc.vector.scalar_tensor_tensor(
                                          coef_tm[:, s0:s1], lg_sb[:, s0:s1], -2.0,
                                          coef_tm[:, s0:s1], ALU.mult, ALU.add))(),
                                      deps=[mk_], inc=True)
                    else:
                        prev = mx
                e8 = sch.op("dve", (lambda s0=s0, s1=s1: lambda:
                            nc.vector.tensor_single_scalar(
                                coef_tm[:, s0:s1], wrk_sb[:, s0:s1], fcols[:, 3:4],
                                ALU.is_ge))(), deps=[prev], inc=True)
                e9 = sch.op("dve", (lambda s0=s0, s1=s1: lambda: nc.vector.tensor_mul(
                    coef_tm[:, s0:s1], coef_tm[:, s0:s1], wrk_sb[:, s0:s1]))(),
                    deps=[e8], inc=True)
                cevs.append(e9)
            fev = None
            for m8 in range(8):
                tev = sch.op("pe", (lambda m8=m8: lambda: nc.tensor.transpose(
                    p512[1][0:16, 0:128], coef_tm[:, m8 * 16:(m8 + 1) * 16],
                    identf[:]))(),
                    deps=[cevs[m8]] + sch.wr(pb[1]), inc=True)
                sch.set_ready(pb[1], tev)
                fev = sch.op("dve", (lambda m8=m8: lambda: nc.vector.tensor_copy(
                    coef_fm[0:16, m8 * 128:(m8 + 1) * 128], p512[1][0:16, 0:128]))(),
                    deps=sch.rd(pb[1]) + (sch.wr(cfm_b) if m8 == 0 else []), inc=True)
                sch.add_reader(pb[1], fev)
            sch.set_ready(cfm_b, fev)
            for tg in range(2):
                bank, bb_ = p512[1], pb[1]
                sgev = None
                for ci in range(16):
                    sgev = sch.op("pe", (lambda ci=ci, tg=tg, bank=bank: lambda:
                                  nc.tensor.matmul(
                                      bank[0:1, 0:512],
                                      sgw_sb[:, ci:ci + 1], x2_slice(ci, tg),
                                      start=(ci == 0), stop=(ci == 15)))(),
                                  deps=(sch.wr(bb_) + sch.rd(sgw_b) + list(x2_ready))
                                  if ci == 0 else (), inc=(ci == 15))
                sch.set_ready(bb_, sgev)
                ev = sch.op("act", (lambda tg=tg: lambda: nc.scalar.activation(
                    sg_row[0:1, tg * 512:(tg + 1) * 512], p512[1][0:1, 0:512],
                    AF.Sigmoid))(),
                    deps=sch.rd(pb[1]) + (sch.wr(sgrow_b) if tg == 0 else []), inc=True)
                sch.add_reader(pb[1], ev)
            sch.set_ready(sgrow_b, ev)
            for tg in range(2):
                bank, bb_ = p512[1], pb[1]
                bev = sch.op("pe", (lambda tg=tg, bank=bank: lambda: nc.tensor.matmul(
                    bank[:], ones_f32[0:1, :], sg_row[0:1, tg * 512:(tg + 1) * 512],
                    start=True, stop=True))(),
                    deps=sch.wr(bb_) + sch.rd(sgrow_b), inc=True)
                sch.set_ready(bb_, bev)
                ev = sch.op("dve", (lambda tg=tg: lambda: nc.vector.tensor_copy(
                    sgr_sb[:, tg * 512:(tg + 1) * 512], p512[1][:]))(),
                    deps=sch.rd(pb[1]) + (sch.wr(sgr_b) if tg == 0 else []), inc=True)
                sch.add_reader(pb[1], ev)
            sch.set_ready(sgr_b, ev)

        def gu_pair(ip, s0, s1, x2_ready, rep_b, rep_sb):
            hev = None
            for tg in range(2):
                gbank, gb_ = p512[2 * tg], pb[2 * tg]
                ubank, ub_ = p512[2 * tg + 1], pb[2 * tg + 1]
                mg = None
                for ci in range(16):
                    mg = sch.op("pe", (lambda ci=ci, tg=tg, gbank=gbank, s0=s0: lambda:
                                nc.tensor.matmul(
                                    gbank[:], wblk[s0][:, ci * 128:(ci + 1) * 128],
                                    x2_slice(ci, tg), start=(ci == 0), stop=(ci == 15)))(),
                                deps=(sch.wr(gb_) + sch.rd(wblk_b[s0]) + list(x2_ready))
                                if ci == 0 else (), inc=(ci == 15))
                sch.set_ready(gb_, mg)
                sch.add_reader(wblk_b[s0], mg)
                mu = None
                for ci in range(16):
                    mu = sch.op("pe", (lambda ci=ci, tg=tg, ubank=ubank, s1=s1: lambda:
                                nc.tensor.matmul(
                                    ubank[:], wblk[s1][:, ci * 128:(ci + 1) * 128],
                                    x2_slice(ci, tg), start=(ci == 0), stop=(ci == 15)))(),
                                deps=(sch.wr(ub_) + sch.rd(wblk_b[s1]))
                                if ci == 0 else (), inc=(ci == 15))
                sch.set_ready(ub_, mu)
                sch.add_reader(wblk_b[s1], mu)
                ge = sch.op("act", (lambda tg=tg, gbank=gbank: lambda:
                            nc.scalar.activation(
                                gs_sb[:, tg * 512:(tg + 1) * 512], gbank[:], AF.Silu))(),
                            deps=sch.rd(gb_) + (sch.wr(gs_b) if tg == 0 else []),
                            inc=True)
                sch.add_reader(gb_, ge)
                ue = sch.op("dve", (lambda tg=tg, ubank=ubank, rep_sb=rep_sb: lambda:
                            nc.vector.tensor_mul(
                                us_sb[:, tg * 512:(tg + 1) * 512], ubank[:],
                                rep_sb[:, tg * 512:(tg + 1) * 512]))(),
                            deps=sch.rd(ub_, rep_b) + (sch.wr(us_b) if tg == 0 else []),
                            inc=True)
                sch.add_reader(ub_, ue)
                hev = sch.op("dve", (lambda ip=ip, tg=tg: lambda: nc.vector.tensor_mul(
                    h_sb[:, ip * 1024 + tg * 512: ip * 1024 + (tg + 1) * 512],
                    gs_sb[:, tg * 512:(tg + 1) * 512],
                    us_sb[:, tg * 512:(tg + 1) * 512]))(),
                    deps=[ge, ue] + (sch.wr(h_b) if (ip == 0 and tg == 0) else []),
                    inc=True)
            sch.add_reader(gs_b, hev)
            sch.add_reader(us_b, hev)
            return hev

        acc_last = [None] * 32

        def down_16(wtensor, wbufs, load_fn, h_ready, first_flags):
            for o in range(16):
                s = o % 2
                lw = sch.dma("pool", (lambda o=o, s=s: lambda: load_fn(o, s))(),
                             deps=sch.wr(wbufs[s]), sem=f"d_dw{s}")
                sch.set_ready(wbufs[s], lw)
                for tg in range(2):
                    bank, bb_ = p512[4 + tg], pb[4 + tg]
                    mev = None
                    for ip in range(11):
                        mev = sch.op("pe", (lambda o=o, ip=ip, tg=tg, bank=bank,
                                     wt=wtensor, s=s: lambda: nc.tensor.matmul(
                                         bank[:], wt[s][:, ip * 128:(ip + 1) * 128],
                                         h_sb[:, ip * 1024 + tg * 512: ip * 1024 + (tg + 1) * 512],
                                         start=(ip == 0), stop=(ip == 10)))(),
                                     deps=(sch.wr(bb_) + [lw] + list(h_ready))
                                     if ip == 0 else (), inc=(ip == 10))
                    sch.set_ready(bb_, mev)
                    sch.add_reader(wbufs[s], mev)
                    sch.add_reader(h_b, mev)
                    ot = o * 1024 + tg * 512
                    idx = o * 2 + tg
                    chain = [acc_last[idx]] if acc_last[idx] is not None else []
                    if first_flags[idx]:
                        first_flags[idx] = False
                        war = acc_b.readers + chain
                        acc_b.readers = []
                        ev = sch.op("dve", (lambda ot=ot, bank=bank: lambda:
                                    nc.vector.tensor_copy(acc[:, ot:ot + 512], bank[:]))(),
                                    deps=sch.rd(bb_) + war, inc=True)
                    else:
                        ev = sch.op("dve", (lambda ot=ot, bank=bank: lambda:
                                    nc.vector.tensor_add(
                                        acc[:, ot:ot + 512], acc[:, ot:ot + 512],
                                        bank[:]))(),
                                    deps=sch.rd(bb_) + chain, inc=True)
                    acc_last[idx] = ev
                    sch.add_reader(bb_, ev)

        def ffn_half(hf, x2_ready, mid_cb=None):
            x2h_free = [None]
            router(hf, x2_ready)
            first_flags = [True] * 32
            for ig in range(4):
                for ip in range(11):
                    i = ig * 11 + ip
                    s0, s1 = 2 * (ip % 2), 2 * (ip % 2) + 1
                    le0 = sch.dma("pool", (lambda i=i, s0=s0: lambda: nc.gpsimd.dma_start(
                        out=wblk[s0][:], in_=sgu_hbm[i].rearrange("p c o -> p (c o)")))(),
                        deps=sch.wr(wblk_b[s0]), sem=f"d_wb{s0}")
                    sch.set_ready(wblk_b[s0], le0)
                    le1 = sch.dma("pool", (lambda i=i, s1=s1: lambda: nc.gpsimd.dma_start(
                        out=wblk[s1][:], in_=sgu_hbm[i + 44].rearrange("p c o -> p (c o)")))(),
                        deps=sch.wr(wblk_b[s1]), sem=f"d_wb{s1}")
                    sch.set_ready(wblk_b[s1], le1)
                    gu_pair(ip, s0, s1, x2_ready, sgr_b, sgr_sb)
                h_ready = [Ev(("s_dve", sch.cnt["s_dve"]))]

                def sdw_load(o, s, ig=ig):
                    return nc.gpsimd.dma_start(
                        out=sdwblk[s][:],
                        in_=sdw_hbm[o][:, ig * 11:(ig + 1) * 11, :]
                        .rearrange("p c o2 -> p (c o2)"))
                down_16(sdwblk, sdwblk_b, sdw_load, h_ready, first_flags)
            for e in range(NE):
                set_onehot(e)
                crev = None
                for tg in range(2):
                    bank, bb_ = p512[1], pb[1]
                    bev = sch.op("pe", (lambda tg=tg, bank=bank: lambda:
                                 nc.tensor.matmul(
                                     bank[:], onehot_sb[:],
                                     coef_fm[0:16, tg * 512:(tg + 1) * 512],
                                     start=True, stop=True))(),
                                 deps=sch.wr(bb_) + sch.rd(cfm_b, onehot_b), inc=True)
                    sch.add_reader(onehot_b, bev)
                    sch.set_ready(bb_, bev)
                    crev = sch.op("dve", (lambda tg=tg: lambda: nc.vector.tensor_copy(
                        cr_sb[:, tg * 512:(tg + 1) * 512], p512[1][:]))(),
                        deps=sch.rd(pb[1]) + (sch.wr(cr_b) if tg == 0 else []), inc=True)
                    sch.add_reader(pb[1], crev)
                sch.set_ready(cr_b, crev)
                for op_ in range(11):
                    s0, s1 = 2 * (op_ % 2), 2 * (op_ % 2) + 1
                    le0 = sch.dma("pool", (lambda e=e, op_=op_, s0=s0: lambda:
                                  nc.gpsimd.dma_start(
                                      out=wblk[s0][:],
                                      in_=w13_hbm[e, op_].rearrange("p c o -> p (c o)")))(),
                                  deps=sch.wr(wblk_b[s0]), sem=f"d_wb{s0}")
                    sch.set_ready(wblk_b[s0], le0)
                    le1 = sch.dma("pool", (lambda e=e, op_=op_, s1=s1: lambda:
                                  nc.gpsimd.dma_start(
                                      out=wblk[s1][:],
                                      in_=w13_hbm[e, op_ + 11].rearrange("p c o -> p (c o)")))(),
                                  deps=sch.wr(wblk_b[s1]), sem=f"d_wb{s1}")
                    sch.set_ready(wblk_b[s1], le1)
                    gu_pair(op_, s0, s1, x2_ready, cr_b, cr_sb)
                h_ready = [Ev(("s_dve", sch.cnt["s_dve"]))]
                if e == NE - 1:
                    x2h_free[0] = Ev(("s_pe", sch.cnt["s_pe"]))

                def w2_load(o, s, e=e):
                    return nc.gpsimd.dma_start(
                        out=w2blk[s][:], in_=w2_hbm[e, o].rearrange("p c o2 -> p (c o2)"))
                down_16(w2blk, w2blk_b, w2_load, h_ready, first_flags)
            acc_ready = Ev(("s_dve", sch.cnt["s_dve"]))
            if mid_cb is not None:
                mid_cb(x2h_free[0])
            for mt in range(8):
                m = hf * 8 + mt
                aslot = mt % 2
                srcs = [acc[:, o * 1024 + mt * 128: o * 1024 + (mt + 1) * 128]
                        for o in range(16)]
                dsts = [acct[aslot][:, o * 128:(o + 1) * 128] for o in range(16)]
                accdeps = [acc_ready] + [e_ for e_ in acc_last if e_ is not None]
                tevs = transpose_tiles(srcs, dsts,
                                       accdeps + sch.wr(acct_b[aslot]), dtype="f32")
                for e_ in tevs:
                    sch.add_reader(acc_b, e_)
                sch.set_ready(acct_b[aslot], tevs[-1])
                tready = [Ev(("s_act", sch.cnt["s_act"])), Ev(("s_dve", sch.cnt["s_dve"]))]
                xslot = mt % 2
                x1l = sch.dma("sp", (lambda m=m, ss=xslot: lambda: nc.sync.dma_start(
                    out=x1f[ss][:], in_=x1_dram[m * 128:(m + 1) * 128, :]))(),
                    deps=sch.wr(x1f_b[xslot]), sem=f"d_xf{xslot}")
                sch.set_ready(x1f_b[xslot], x1l)
                c0 = 8 + (mt % 2) * 4
                e1 = sch.op("dve", (lambda aslot=aslot, c0=c0: lambda:
                            nc.vector.reduce_sum(
                                fcols[:, c0:c0 + 1], acct[aslot][:], axis=AX.X))(),
                            deps=tready, inc=True)
                e2 = sch.op("dve", (lambda c0=c0: lambda: nc.vector.tensor_scalar_mul(
                    fcols[:, c0 + 1:c0 + 2], fcols[:, c0:c0 + 1], 1.0 / DM))(),
                    deps=[e1], inc=True)
                e3 = sch.op("dve", (lambda aslot=aslot, c0=c0: lambda:
                            nc.vector.tensor_single_scalar(
                                acct[aslot][:], acct[aslot][:],
                                fcols[:, c0 + 1:c0 + 2], ALU.subtract))(),
                            deps=[e2], inc=True)
                e4 = sch.op("act", (lambda aslot=aslot, c0=c0: lambda:
                            nc.scalar.activation(
                                fdump[:], acct[aslot][:], AF.Square,
                                accum_out=fcols[:, c0 + 2:c0 + 3]))(),
                            deps=[e3], inc=True)
                e5 = sch.op("act", (lambda c0=c0: lambda: nc.scalar.activation(
                    fcols[:, c0 + 3:c0 + 4], fcols[:, c0 + 2:c0 + 3], AF.Sqrt,
                    bias=EPS_LN, scale=1.0 / DM))(), deps=[e4], inc=True)
                e6 = sch.op("dve", (lambda c0=c0: lambda: nc.vector.reciprocal(
                    fcols[:, c0 + 2:c0 + 3], fcols[:, c0 + 3:c0 + 4]))(),
                    deps=[e5], inc=True)
                e7 = sch.op("dve", (lambda aslot=aslot, xslot=xslot, c0=c0: lambda:
                            nc.vector.scalar_tensor_tensor(
                                x1f[xslot][:], acct[aslot][:],
                                fcols[:, c0 + 2:c0 + 3], x1f[xslot][:],
                                ALU.mult, ALU.add))(),
                            deps=[e6] + sch.rd(x1f_b[xslot]), inc=True)
                sch.add_reader(acct_b[aslot], e7)
                sch.set_ready(x1f_b[xslot], e7)
                ode = sch.dma("sp", (lambda m=m, ss=xslot: lambda: nc.sync.dma_start(
                    out=out_hbm[m * 128:(m + 1) * 128, :], in_=x1f[ss][:]))(),
                    deps=sch.rd(x1f_b[xslot]), sem="d_out")
                sch.add_reader(x1f_b[xslot], ode)
            return Ev(("s_pe", sch.cnt["s_pe"]))

        x2h1_ready_box = [None]

        def rebuild_x2h(x2h_free_ev):
            for m8 in range(8):
                slot = m8 % 2
                ld = sch.dma("sp", (lambda m8=m8, ss=slot: lambda: nc.sync.dma_start(
                    out=xrb[ss][:], in_=x2b_dram[m8]))(),
                    deps=sch.wr(xrb_b[slot]), sem=f"d_rb{slot}")
                sch.set_ready(xrb_b[slot], ld)
                srcs = [xrb[slot][:, ci * 128:(ci + 1) * 128] for ci in range(16)]
                dsts = [x2h_fm[:, ci * 1024 + m8 * 128: ci * 1024 + (m8 + 1) * 128]
                        for ci in range(16)]
                evs = transpose_tiles(srcs, dsts,
                                      sch.rd(xrb_b[slot]) + [x2h_free_ev])
                for e_ in evs:
                    sch.add_reader(xrb_b[slot], e_)
            x2h1_ready_box[0] = [Ev(("s_act", sch.cnt["s_act"])),
                                 Ev(("s_dve", sch.cnt["s_dve"]))]

        ffn_half(0, x2h_ready, mid_cb=rebuild_x2h)
        ffn_half(1, x2h1_ready_box[0])

        sch.op("sp", lambda: None, deps=sch.snapshot())

        sems = {}
        for s in sorted(sch.sem_names):
            sems[s] = E(nc.semaphore(s))

        block = E(nc.Block())

        def runner(engname):
            def run(eng):
                for it in sch.items[engname]:
                    if it[0] == "wait":
                        eng.wait_ge(sems[it[1]], it[2])
                    else:
                        _, fn, s, n = it
                        inst = fn()
                        if s is not None and inst is not None:
                            inst.then_inc(sems[s], n)
            return run

        block.tensor(runner("pe"))
        block.vector(runner("dve"))
        block.scalar(runner("act"))
        block.gpsimd(runner("pool"))
        block.sync(runner("sp"))
        ffn_ctx.close()

    return nc


def _prep_weights(i):
    d = {}
    d["wq"] = np.ascontiguousarray(i["L0_attn_Wq"].T.reshape(16, 128, DM))
    d["wk"] = np.ascontiguousarray(i["L0_attn_Wk"].T.reshape(16, 128, 512))
    d["wv"] = np.ascontiguousarray(i["L0_attn_Wv"].T.reshape(16, 128, 512))
    d["wg"] = np.ascontiguousarray(i["L0_attn_Wg"].T.reshape(16, 128, DM))
    d["wmb"] = np.ascontiguousarray(i["L0_attn_Wmb"].T.reshape(16, 128, DM))
    d["biases"] = np.ascontiguousarray(np.concatenate(
        [i["L0_attn_bq"], i["L0_attn_bg"], i["L0_attn_bmb"],
         i["L0_attn_bk"], i["L0_attn_bv"]]).reshape(1, -1))
    d["mem"] = np.ascontiguousarray(i["L0_attn_memory"].reshape(HQ, NFC, 128, HD))
    d["normf"] = np.ascontiguousarray(i["L0_attn_norm"])
    d["gatew"] = np.ascontiguousarray(i["L0_ffn_gate_w"].T.reshape(16, 128, NE))
    d["sgatew"] = np.ascontiguousarray(i["L0_ffn_sgate_w"].reshape(16, 128).T)
    d["w13"] = np.ascontiguousarray(
        i["L0_ffn_w13"].reshape(NE, 22, 128, 16, 128).transpose(0, 1, 4, 3, 2))
    d["w2"] = np.ascontiguousarray(
        i["L0_ffn_w2"].reshape(NE, 16, 128, 11, 128).transpose(0, 1, 4, 3, 2))
    d["sgu"] = np.ascontiguousarray(
        i["L0_ffn_shared_gate_up_w"].reshape(88, 128, 16, 128).transpose(0, 3, 2, 1))
    d["sdw"] = np.ascontiguousarray(
        i["L0_ffn_shared_down_w"].reshape(16, 128, 44, 128).transpose(0, 3, 2, 1))
    return d


def kernel(**inputs):
    x = np.asarray(inputs["L0_attn_hidden_states"])
    w = _prep_weights({k: np.asarray(v) for k, v in inputs.items()})
    nc = build_program()
    in_maps = []
    for c in range(8):
        m = dict(w)
        m["x"] = np.ascontiguousarray(x[c])
        in_maps.append(m)
    res = run_bass_kernel_spmd(nc, in_maps, core_ids=list(range(8)))
    global _LAST_RES
    _LAST_RES = res
    return np.stack([res.results[c]["out"] for c in range(8)], axis=0)


_LAST_RES = None


# version 05
# speedup vs baseline: 1.0460x; 1.0460x over previous
import numpy as np
from collections import defaultdict
from contextlib import ExitStack

import concourse.bass as bass
import concourse.mybir as mybir
from concourse.bass_utils import run_bass_kernel_spmd

F32 = mybir.dt.float32
BF16 = mybir.dt.bfloat16
AF = mybir.ActivationFunctionType
ALU = mybir.AluOpType
AX = mybir.AxisListType

T = 2048
DM = 2048
NT = 16
HQ, HKV, HD = 16, 4, 128
FD = 768
NFC = 6
NU = 3
EPS_LN = 1e-5
EPS_AT = 1e-8
NE, IE = 16, 1408
ISH = 5632
MNP = 129

DEBUG_TAPS = {}

ENGINES = ("pe", "dve", "act", "pool", "sp")


class Ev(tuple):
    __slots__ = ()


class Buf:
    __slots__ = ("ready", "readers")

    def __init__(self):
        self.ready = None
        self.readers = []


class Sched:
    def __init__(self):
        self.items = {e: [] for e in ENGINES}
        self.cnt = defaultdict(int)
        self.waited = {e: defaultdict(int) for e in ENGINES}
        self.sem_names = set()
        self.eng_sem = {e: "s_" + e for e in ENGINES}
        for e in ENGINES:
            self.sem_names.add("s_" + e)
        self.floor = []

    def _wait(self, eng, ev):
        if ev is None:
            return
        s, v = ev
        if self.waited[eng][s] >= v:
            return
        self.waited[eng][s] = v
        self.items[eng].append(("wait", s, v))

    def op(self, eng, fn, deps=(), inc=False, sem=None, ninc=1):
        for d in self.floor:
            self._wait(eng, d)
        for d in deps:
            self._wait(eng, d)
        ev = None
        if inc:
            s = sem or self.eng_sem[eng]
            self.sem_names.add(s)
            self.cnt[s] += ninc
            ev = Ev((s, self.cnt[s]))
            self.items[eng].append(("emit", fn, s, ninc))
        else:
            self.items[eng].append(("emit", fn, None, 0))
        return ev

    def dma(self, eng, fn, deps=(), sem="dma0"):
        return self.op(eng, fn, deps=deps, inc=True, sem=sem, ninc=16)

    @staticmethod
    def rd(*bufs):
        return [b.ready for b in bufs if b.ready is not None]

    @staticmethod
    def wr(*bufs):
        deps = []
        for b in bufs:
            deps += b.readers
            if b.ready is not None:
                deps.append(b.ready)
        return deps

    @staticmethod
    def set_ready(buf, ev):
        buf.ready = ev
        buf.readers = []

    @staticmethod
    def add_reader(buf, ev):
        if ev is not None:
            buf.readers.append(ev)

    def snapshot(self):
        return [Ev((s, self.cnt[s])) for s in list(self.cnt) if self.cnt[s] > 0]

    def set_floor(self):
        self.floor = self.snapshot()


def build_program():
    nc = bass.Bass("TRN2", target_bir_lowering=False)
    sch = Sched()

    def din(name, shape, dtype=F32):
        return nc.dram_tensor(name, list(shape), dtype, kind="ExternalInput").ap()

    x_hbm = din("x", [T, DM])
    wq_hbm = din("wq", [16, 128, DM])
    wk_hbm = din("wk", [16, 128, 512])
    wv_hbm = din("wv", [16, 128, 512])
    wg_hbm = din("wg", [16, 128, DM])
    wmb_hbm = din("wmb", [16, 128, DM])
    bias_hbm = din("biases", [1, 3 * DM + 2 * 512])
    mem_hbm = din("mem", [HQ, NFC, 128, HD])
    normf_hbm = din("normf", [HQ, FD])
    gatew_hbm = din("gatew", [16, 128, NE])
    sgatew_hbm = din("sgatew", [128, 16])
    w13_hbm = din("w13", [NE, 22, 128, 16, 128])
    w2_hbm = din("w2", [NE, 16, 128, 11, 128])
    sgu_hbm = din("sgu", [88, 128, 16, 128])
    sdw_hbm = din("sdw", [16, 128, 44, 128])

    out_hbm = nc.dram_tensor("out", [T, DM], F32, kind="ExternalOutput").ap()

    taps = {}
    for tname, tshape in DEBUG_TAPS.items():
        taps[tname] = nc.dram_tensor("tap_" + tname, list(tshape), F32,
                                     kind="ExternalOutput").ap()

    aout_dram = nc.dram_tensor("aout_scr", [T, DM], F32, kind="Internal").ap()
    x1_dram = nc.dram_tensor("x1_scr", [T, DM], F32, kind="Internal").ap()
    x2b_dram = nc.dram_tensor("x2b_scr", [8, 128, DM], BF16, kind="Internal").ap()
    DSZ = FD * HD + FD
    dAB_in = [nc.dram_tensor(f"d{i}_in", [8, DSZ], F32, kind="Internal").ap()
              for i in range(2)]
    dAB_out = [nc.dram_tensor(f"d{i}_out", [8, DSZ], F32, kind="Internal").ap()
               for i in range(2)]

    BQ0, BG0, BMB0, BK0, BV0 = 0, DM, 2 * DM, 3 * DM, 3 * DM + 512

    main_ctx = ExitStack()
    with main_ctx as ctx:
        E = ctx.enter_context

        def sb(name, shape, dtype, ectx=None):
            return (ectx or ctx).enter_context(nc.sbuf_tensor(name, list(shape), dtype))

        def psum(name, shape, dtype=F32):
            return E(nc.psum_tensor(name, list(shape), dtype))

        ident = sb("ident", [128, 128], BF16)
        identf = sb("identf", [128, 128], F32)
        ones_row = sb("ones_row", [1, 128], BF16)
        m1_sb = sb("m1_sb", [128, NT], F32)
        r1_sb = sb("r1_sb", [128, NT], F32)
        s2_sb = sb("s2_sb", [128, NT], F32)
        q2_sb = sb("q2_sb", [128, NT], F32)
        m2_sb = sb("m2_sb", [128, NT], F32)
        r2_sb = sb("r2_sb", [128, NT], F32)
        m3_sb = sb("m3_sb", [128, NT], F32)
        r3_sb = sb("r3_sb", [128, NT], F32)
        cols = sb("cols_scr", [128, 16], F32)
        CONSTS = [EPS_LN, 1.0 / DM, 1.0, -1.0, EPS_AT]
        const_sb = sb("const_sb", [128, len(CONSTS)], F32)
        for j, cv in enumerate(CONSTS):
            nc.const_aps.register(F32, cv, const_sb[:, j:j + 1])
        C_ONE = const_sb[:, 2:3]

        p512 = [psum(f"p512_{i}", [128, 512], F32) for i in range(6)]
        pb = [Buf() for _ in p512]
        tp_ps = [psum(f"tp{i}", [128, 1024], BF16) for i in range(2)]
        tp_b = [Buf() for _ in tp_ps]

        def init_consts():
            ev = sch.op("pool", lambda: nc.gpsimd.memset(ident[:], 0.0), inc=True)
            sch.op("pool", lambda: nc.gpsimd.affine_select(
                out=ident[:], in_=ident[:], compare_op=ALU.not_equal, fill=1.0,
                base=0, pattern=[[-1, 128]], channel_multiplier=1), deps=[ev])
            ev = sch.op("pool", lambda: nc.gpsimd.memset(identf[:], 0.0), inc=True)
            sch.op("pool", lambda: nc.gpsimd.affine_select(
                out=identf[:], in_=identf[:], compare_op=ALU.not_equal, fill=1.0,
                base=0, pattern=[[-1, 128]], channel_multiplier=1), deps=[ev])
            sch.op("pool", lambda: nc.gpsimd.memset(ones_row[:], 1.0))
            sch.op("pool", lambda: nc.gpsimd.memset(s2_sb[:], 0.0))
            sch.op("pool", lambda: nc.gpsimd.memset(q2_sb[:], 0.0))
            for j, cv in enumerate(CONSTS):
                sch.op("pool", (lambda j=j, cv=cv: lambda: nc.gpsimd.memset(
                    const_sb[:, j:j + 1], cv))())
            return sch.op("pool", lambda: nc.gpsimd.engine_nop() if False else
                          nc.gpsimd.memset(cols[:], 0.0), inc=True)

        consts_ev = init_consts()
        for _e in ENGINES:
            sch._wait(_e, consts_ev)

        def transpose_tiles(srcs, dsts, ev_deps, dtype="bf16", evac_eng_alt=True):
            outevs = []
            n = len(srcs)
            for g4 in range(0, n, 4):
                cnt = min(4, n - g4)
                if dtype == "bf16":
                    ts = (g4 // 4) % 2
                    bank, bankbuf, idn = tp_ps[ts], tp_b[ts], ident
                else:
                    ts = (g4 // 4) % 2
                    bank, bankbuf, idn = p512[ts], pb[ts], identf
                tev = None
                for j in range(cnt):
                    tev = sch.op("pe", (lambda s=srcs[g4 + j], bank=bank, j=j, idn=idn:
                                 lambda: nc.tensor.transpose(
                                     bank[:, j * 128:(j + 1) * 128], s, idn[:]))(),
                                 deps=(sch.wr(bankbuf) + list(ev_deps)) if j == 0 else (),
                                 inc=(j == cnt - 1))
                sch.set_ready(bankbuf, tev)
                eng = "act" if (not evac_eng_alt or (g4 // 4) % 2 == 0) else "dve"
                eev = None
                for j in range(cnt):
                    def mkf(d=dsts[g4 + j], bank=bank, j=j, eng=eng):
                        if eng == "act":
                            return lambda: nc.scalar.copy(d, bank[:, j * 128:(j + 1) * 128])
                        return lambda: nc.vector.tensor_copy(d, bank[:, j * 128:(j + 1) * 128])
                    eev = sch.op(eng, mkf(), deps=sch.rd(bankbuf) if j == 0 else (),
                                 inc=(j == cnt - 1))
                sch.add_reader(bankbuf, eev)
                outevs.append(eev)
            return outevs

        at_ctx = ExitStack()
        xin_fm = sb("xin_fm", [128, 16 * T], BF16, at_ctx)
        xt = [sb(f"xt{i}", [128, DM], F32, at_ctx) for i in range(2)]
        xt_b = [Buf() for _ in xt]
        xbf = [sb(f"xbf{i}", [128, DM], BF16, at_ctx) for i in range(2)]
        xbf_b = [Buf() for _ in xbf]

        def ln_stats_and_scale(xs_ap, dump_ap, mdst, rdst, c0, deps, out_ap, out_ev_needed=True):
            W = xs_ap.shape[-1]
            e1 = sch.op("dve", lambda: nc.vector.reduce_sum(
                cols[:, c0:c0 + 1], xs_ap, axis=AX.X), deps=deps, inc=True)
            e2 = sch.op("dve", lambda: nc.vector.tensor_scalar_mul(
                mdst, cols[:, c0:c0 + 1], 1.0 / W), deps=[e1], inc=True)
            e3 = sch.op("dve", lambda: nc.vector.tensor_single_scalar(
                xs_ap, xs_ap, mdst, ALU.subtract), deps=[e2], inc=True)
            e4 = sch.op("act", lambda: nc.scalar.activation(
                dump_ap, xs_ap, AF.Square, accum_out=cols[:, c0 + 1:c0 + 2]),
                deps=[e3], inc=True)
            e5 = sch.op("act", lambda: nc.scalar.activation(
                cols[:, c0 + 2:c0 + 3], cols[:, c0 + 1:c0 + 2], AF.Sqrt,
                bias=EPS_LN, scale=1.0 / W), deps=[e4], inc=True)
            e6 = sch.op("dve", lambda: nc.vector.reciprocal(
                rdst, cols[:, c0 + 2:c0 + 3]), deps=[e5], inc=True)
            e7 = sch.op("dve", lambda: nc.vector.tensor_single_scalar(
                out_ap, xs_ap, rdst, ALU.mult), deps=[e6], inc=True)
            return e7

        xin_ready_by = {}
        for m in range(NT):
            slot = m % 2
            b = xt_b[slot]
            ld = sch.dma("sp", (lambda mm=m, ss=slot: lambda: nc.sync.dma_start(
                out=xt[ss][:], in_=x_hbm[mm * 128:(mm + 1) * 128, :]))(),
                deps=sch.wr(b), sem=f"d_x{slot}")
            sch.set_ready(b, ld)
            bb = xbf_b[slot]
            e7 = ln_stats_and_scale(xt[slot][:], xbf[slot][:],
                                    m1_sb[:, m:m + 1], r1_sb[:, m:m + 1],
                                    (m % 2) * 4,
                                    sch.rd(b) + sch.wr(bb), xbf[slot][:])
            sch.set_ready(bb, e7)
            sch.add_reader(b, e7)
            srcs = [xbf[slot][:, ci * 128:(ci + 1) * 128] for ci in range(16)]
            dsts = [xin_fm[:, ci * T + m * 128: ci * T + (m + 1) * 128] for ci in range(16)]
            evs = transpose_tiles(srcs, dsts, sch.rd(bb))
            for e in evs:
                sch.add_reader(bb, e)
            if m == 7 or m == 15:
                xin_ready_by[m // 8] = [Ev(("s_act", sch.cnt["s_act"])),
                                        Ev(("s_dve", sch.cnt["s_dve"]))]

        memnorm = sb("memnorm", [128, HQ * NFC * MNP], BF16, at_ctx)
        norm_tt = sb("norm_tt", [128, NFC * 16], F32, at_ctx)
        norm16 = sb("norm16", [16, FD], F32, at_ctx)
        wkg = sb("wkg", [128, 16 * 128], BF16, at_ctx)
        wvg = sb("wvg", [128, 16 * 128], BF16, at_ctx)
        wmbg = [sb(f"wmbg{i}", [128, 16 * 128], BF16, at_ctx) for i in range(2)]
        r_sb = sb("r_sb", [128, 8 * 256], F32, at_ctx)
        v_sb = sb("v_sb", [128, 8 * 128], F32, at_ctx)
        mkraw = sb("mkraw", [128, 8 * FD], BF16, at_ctx)
        mkfm = sb("mkfm", [128, NFC * 1024], BF16, at_ctx)
        mb_sb = [sb(f"mb_sb{i}", [128, 8 * 128], F32, at_ctx) for i in range(2)]
        wmv_sb = [sb(f"wmv{i}", [128, 8 * 128], BF16, at_ctx) for i in range(2)]
        dsb = [sb(f"dsb{i}", [128, FD], F32, at_ctx) for i in range(1)]
        dnsb = sb("dnsb", [4, FD], F32, at_ctx)
        coefp = sb("coefp", [128, 8 * 4], BF16, at_ctx)
        kc = sb("kc", [128, 64], F32, at_ctx)
        sqdump = sb("sqdump", [128, FD], BF16, at_ctx)
        t2_sb = sb("t2_sb", [128, 2 * 128], F32, at_ctx)
        g_sb = [sb(f"g_sb{i}", [128, 8 * 128], F32, at_ctx) for i in range(2)]
        xc_sb = sb("xc_sb", [128, 2 * 128], F32, at_ctx)
        aout_sb = [sb(f"aout_sb{i}", [128, 8 * 128], F32, at_ctx) for i in range(1)]
        mstg = dsb[0]
        dstg = sb("dstg", [128, FD], F32, at_ctx)
        bslot = [sb(f"bslot{i}", [1, 128], BF16, at_ctx) for i in range(2)]

        wkg_b, wvg_b = Buf(), Buf()
        wmbg_b = [Buf(), Buf()]
        r_b, v_b = Buf(), Buf()
        mkraw_b, mkfm_b = Buf(), Buf()
        mb_b = [Buf(), Buf()]
        wmv_b = [Buf(), Buf()]
        dsb_b = [Buf()]
        dnsb_b = Buf()
        t2_b = [Buf(), Buf()]
        g_b = [Buf(), Buf()]
        xc_b = [Buf(), Buf()]
        aout_b = [Buf()]
        mstg_b, dstg_b = dsb_b[0], Buf()
        bslot_b = [Buf(), Buf()]
        memnorm_b = [Buf(), Buf()]
        ntt_b = Buf()
        coefp_b = Buf()
        xt2_b = xt_b

        def build_memnorm():
            mev = None
            for h in range(HQ):
                dst = memnorm[:, h * NFC * MNP:(h + 1) * NFC * MNP]
                dst = dst.rearrange("p (f c) -> p f c", c=MNP)[:, :, 0:128]
                src = mem_hbm[h].rearrange("f p d -> p f d")
                mev = sch.dma("pool", (lambda dst=dst, src=src: lambda:
                              nc.gpsimd.dma_start(out=dst, in_=src))(), sem="d_mem")
            nev = sch.dma("sp", lambda: nc.sync.dma_start(out=norm16[:], in_=normf_hbm),
                          sem="d_mem2")
            srcs = [norm16[:, fc * 128:(fc + 1) * 128] for fc in range(NFC)]
            tev = None
            for fc in range(NFC):
                tev = sch.op("pe", (lambda fc=fc: lambda: nc.tensor.transpose(
                    p512[5][:, fc * 16:(fc + 1) * 16], norm16[:, fc * 128:(fc + 1) * 128],
                    identf[0:16, 0:16]))(),
                    deps=([nev] + sch.wr(pb[5])) if fc == 0 else (), inc=(fc == NFC - 1))
            sch.set_ready(pb[5], tev)
            cev = sch.op("dve", lambda: nc.vector.tensor_copy(norm_tt[:], p512[5][:, 0:96]),
                         deps=sch.rd(pb[5]), inc=True)
            sch.add_reader(pb[5], cev)
            sch.set_ready(ntt_b, cev)
            lastc = None
            for fc in range(NFC):
                dst = memnorm[:, 0:HQ * NFC * MNP].rearrange(
                    "p (h c) -> p h c", c=NFC * MNP)[:, :, fc * MNP + 128: fc * MNP + 129]
                lastc = sch.op("dve", (lambda fc=fc, dst=dst: lambda: nc.vector.tensor_copy(
                    dst, norm_tt[:, fc * 16:(fc + 1) * 16].rearrange("p (h o) -> p h o", o=1)))(),
                    deps=[cev] if fc == 0 else (), inc=(fc == NFC - 1))
            return [mev, lastc]

        memnorm_ready = build_memnorm()

        def proj8(th, rhs_fn, bias_off, evac_fn, wbuf, extra_deps=()):
            bs = th % 2
            bev = sch.dma("pool", (lambda bs=bs, bias_off=bias_off: lambda:
                          nc.gpsimd.dma_start(
                              out=bslot[bs][:],
                              in_=bias_hbm[0:1, bias_off:bias_off + 128]))(),
                          deps=sch.wr(bslot_b[bs]), sem=f"d_b{bs}")
            sch.set_ready(bslot_b[bs], bev)
            evs = []
            for half4 in range(2):
                bank, bb_ = p512[half4], pb[half4]
                deps = sch.wr(bb_) + xin_ready_by[th] + sch.rd(wbuf) + [bev] + list(extra_deps)
                mev = None
                for j in range(4):
                    m = th * 8 + half4 * 4 + j
                    for ci in range(16):
                        mev = sch.op("pe", (lambda j=j, m=m, ci=ci, bank=bank: lambda:
                                     nc.tensor.matmul(
                                         bank[:, j * 128:(j + 1) * 128],
                                         xin_fm[:, ci * T + m * 128: ci * T + (m + 1) * 128],
                                         rhs_fn(ci), start=(ci == 0), stop=False))(),
                                     deps=deps if (j == 0 and ci == 0) else ())
                    mev = sch.op("pe", (lambda j=j, bank=bank, bs=bs: lambda:
                                 nc.tensor.matmul(
                                     bank[:, j * 128:(j + 1) * 128],
                                     ones_row[0:1, :], bslot[bs][:],
                                     start=False, stop=True))(),
                                 inc=(j == 3))
                sch.set_ready(bb_, mev)
                sch.add_reader(wbuf, mev)
                sch.add_reader(bslot_b[bs], mev)
                eev = evac_fn(bank, half4, sch.rd(bb_))
                sch.add_reader(bb_, eev)
                evs.append(eev)
            return evs

        def evac_relu_pair(dst_sb, dst_buf):
            def f(bank, half4, deps):
                base = half4 * 4 * 256
                view = dst_sb[:, base:base + 4 * 256].rearrange("p (m c) -> p m c", c=256)
                dst_p = view[:, :, 0:128]
                dst_n = view[:, :, 128:256]
                src = bank[:].rearrange("p (m c) -> p m c", c=128)
                sch.op("act", (lambda dst_p=dst_p, src=src: lambda: nc.scalar.activation(
                    dst_p, src, AF.Relu))(), deps=deps + sch.wr(dst_buf), inc=True)
                e2 = sch.op("act", (lambda dst_n=dst_n, src=src: lambda: nc.scalar.activation(
                    dst_n, src, AF.Relu, scale=-1.0))(), inc=True)
                return e2
            return f

        def evac_copy(dst_sb, dst_buf, func=None):
            def f(bank, half4, deps):
                dst = dst_sb[:, half4 * 512:(half4 + 1) * 512]
                if func is None:
                    return sch.op("act", (lambda dst=dst, bank=bank: lambda:
                                  nc.scalar.copy(dst, bank[:]))(),
                                  deps=deps + sch.wr(dst_buf), inc=True)
                return sch.op("act", (lambda dst=dst, bank=bank: lambda:
                              nc.scalar.activation(dst, bank[:], func))(),
                              deps=deps + sch.wr(dst_buf), inc=True)
            return f

        def dpfp8(kevs, extra_wr=()):
            prod_evs = []
            for mm in range(8):
                pe_ = None
                deps = (sch.wr(mkraw_b) + list(kevs)) if mm == 0 else []
                for j in range(1, NU + 1):
                    base = mm * FD + (j - 1) * 256
                    rb = mm * 256
                    sch.op("dve", (lambda base=base, rb=rb, j=j: lambda:
                           nc.vector.tensor_mul(
                               mkraw[:, base + j: base + 256],
                               r_sb[:, rb + j: rb + 256],
                               r_sb[:, rb: rb + 256 - j]))(),
                           deps=deps if j == 1 else ())
                    pe_ = sch.op("dve", (lambda base=base, rb=rb, j=j: lambda:
                                 nc.vector.tensor_mul(
                                     mkraw[:, base: base + j],
                                     r_sb[:, rb: rb + j],
                                     r_sb[:, rb + 256 - j: rb + 256]))(),
                                 inc=(j == NU))
                prod_evs.append(pe_)
            sch.set_ready(mkraw_b, prod_evs[-1])
            for e in prod_evs:
                sch.add_reader(r_b, e)
            sev = None
            for mm in range(8):
                sev = sch.op("act", (lambda mm=mm: lambda: nc.scalar.activation(
                    sqdump[:], mkraw[:, mm * FD:(mm + 1) * FD], AF.Square,
                    accum_out=kc[:, mm:mm + 1]))(),
                    deps=[prod_evs[mm]] + ([sev] if sev is not None else []),
                    inc=True)
            nev = sch.op("act", lambda: nc.scalar.activation(
                kc[:, 8:16], kc[:, 0:8], AF.Sqrt), deps=[sev], inc=True)
            rnev = sch.op("dve", lambda: nc.vector.reciprocal(kc[:, 16:24], kc[:, 8:16]),
                          deps=[nev], inc=True)
            rqev = sch.op("dve", lambda: nc.vector.reciprocal(kc[:, 24:32], kc[:, 0:8]),
                          deps=[sev], inc=True)
            srcs, dsts = [], []
            for mm in range(8):
                for fc in range(NFC):
                    srcs.append(mkraw[:, mm * FD + fc * 128: mm * FD + (fc + 1) * 128])
                    dsts.append(mkfm[:, fc * 1024 + mm * 128: fc * 1024 + (mm + 1) * 128])
            tevs = transpose_tiles(srcs, dsts,
                                   [prod_evs[-1]] + sch.wr(mkfm_b) + list(extra_wr))
            for e in tevs:
                sch.add_reader(mkraw_b, e)
            fm_ready = [Ev(("s_act", sch.cnt["s_act"])), Ev(("s_dve", sch.cnt["s_dve"]))]
            return fm_ready, [nev, rnev, rqev]

        def numden(qh, mn_deps, fm_ready):
            bank_evs = []
            dcp = None
            for pair in range(4):
                bank, bb_ = p512[2 + pair], pb[2 + pair]
                deps = sch.wr(bb_) + fm_ready + list(mn_deps)
                mev = None
                for j in range(2):
                    mm = pair * 2 + j
                    for fc in range(NFC):
                        mev = sch.op("pe", (lambda j=j, mm=mm, fc=fc, bank=bank, qh=qh:
                                     lambda: nc.tensor.matmul(
                                         bank[:, j * MNP:j * MNP + MNP],
                                         mkfm[:, fc * 1024 + mm * 128: fc * 1024 + (mm + 1) * 128],
                                         memnorm[:, (qh * NFC + fc) * MNP:(qh * NFC + fc + 1) * MNP],
                                         start=(fc == 0), stop=(fc == NFC - 1)))(),
                                     deps=deps if (j == 0 and fc == 0) else (),
                                     inc=(j == 1 and fc == NFC - 1))
                sch.set_ready(bb_, mev)
                sch.add_reader(memnorm_b[qh // 8], mev)
                densrc = bank[:, 0:2 * MNP].rearrange("p (j c) -> p j c", c=MNP)[:, :, 128:129]
                densrc = densrc.rearrange("p j c -> p (j c)")
                dcp = sch.op("dve", (lambda densrc=densrc, pair=pair: lambda:
                             nc.vector.tensor_copy(
                                 kc[:, 32 + 2 * pair:34 + 2 * pair], densrc))(),
                             deps=sch.rd(bb_), inc=True)
                sch.add_reader(bb_, dcp)
                bank_evs.append(mev)
            e1 = sch.op("dve", lambda: nc.vector.scalar_tensor_tensor(
                kc[:, 40:48], kc[:, 8:16], -EPS_AT, kc[:, 32:40],
                ALU.mult, ALU.subtract), deps=[dcp], inc=True)
            e2 = sch.op("dve", lambda: nc.vector.reciprocal(kc[:, 48:56], kc[:, 40:48]),
                        deps=[e1], inc=True)
            return bank_evs, e2

        delta_dma_evs = [[], []]

        def kside_group(g):
            wk_ev = sch.dma("pool", (lambda g=g: lambda: nc.gpsimd.dma_start(
                out=wkg[:].rearrange("p (c o) -> p c o", o=128),
                in_=wk_hbm[:, :, g * 128:(g + 1) * 128]
                .rearrange("c p o -> p c o")))(),
                deps=sch.wr(wkg_b), sem="d_wk")
            sch.set_ready(wkg_b, wk_ev)
            wv_ev = sch.dma("pool", (lambda g=g: lambda: nc.gpsimd.dma_start(
                out=wvg[:].rearrange("p (c o) -> p c o", o=128),
                in_=wv_hbm[:, :, g * 128:(g + 1) * 128]
                .rearrange("c p o -> p c o")))(),
                deps=sch.wr(wvg_b), sem="d_wv")
            sch.set_ready(wvg_b, wv_ev)
            for th in range(2):
                kevs = proj8(th, lambda ci: wkg[:, ci * 128:(ci + 1) * 128],
                             BK0 + g * 128, evac_relu_pair(r_sb, r_b), wkg_b)
                sch.set_ready(r_b, kevs[-1])
                vevs = proj8(th, lambda ci: wvg[:, ci * 128:(ci + 1) * 128],
                             BV0 + g * 128, evac_copy(v_sb, v_b), wvg_b)
                sch.set_ready(v_b, vevs[-1])
                fm_ready, (nev, rnev, rqev) = dpfp8(kevs)
                for q4 in range(4):
                    qh = g * 4 + q4
                    ws = q4 % 2
                    wmb_ev = sch.dma("pool", (lambda qh=qh, ws=ws: lambda:
                                     nc.gpsimd.dma_start(
                                         out=wmbg[ws][:].rearrange("p (c o) -> p c o", o=128),
                                         in_=wmb_hbm[:, :, qh * 128:(qh + 1) * 128]
                                         .rearrange("c p o -> p c o")))(),
                                     deps=sch.wr(wmbg_b[ws]), sem=f"d_wmb{ws}")
                    sch.set_ready(wmbg_b[ws], wmb_ev)
                    mslot = q4 % 2
                    mevs = proj8(th, (lambda ws=ws: lambda ci: wmbg[ws][:, ci * 128:(ci + 1) * 128])(),
                                 BMB0 + qh * 128,
                                 evac_copy(mb_sb[mslot], mb_b[mslot], AF.Sigmoid),
                                 wmbg_b[ws])
                    sch.set_ready(mb_b[mslot], mevs[-1])
                    bank_evs, rden_ev = numden(qh, memnorm_ready, fm_ready)
                    wslot_w = q4 % 2
                    wmv_evs = []
                    for mm in range(8):
                        pair = mm // 2
                        j = mm % 2
                        bank, bb_ = p512[2 + pair], pb[2 + pair]
                        num = bank[:, j * MNP:j * MNP + 128]
                        t2slot = mm % 2
                        e3 = sch.op("dve", (lambda num=num, mm=mm, t2slot=t2slot: lambda:
                                    nc.vector.scalar_tensor_tensor(
                                        t2_sb[:, t2slot * 128:(t2slot + 1) * 128],
                                        num, kc[:, 48 + mm:49 + mm],
                                        v_sb[:, mm * 128:(mm + 1) * 128],
                                        ALU.mult, ALU.add))(),
                                    deps=[rden_ev] + sch.rd(v_b, mb_b[mslot])
                                    + sch.wr(t2_b[t2slot]), inc=True)
                        sch.set_ready(t2_b[t2slot], e3)
                        e4 = sch.op("dve", (lambda mm=mm, t2slot=t2slot, ws=wslot_w: lambda:
                                    nc.vector.scalar_tensor_tensor(
                                        wmv_sb[ws][:, mm * 128:(mm + 1) * 128],
                                        t2_sb[:, t2slot * 128:(t2slot + 1) * 128],
                                        kc[:, 16 + mm:17 + mm],
                                        mb_sb[mslot][:, mm * 128:(mm + 1) * 128],
                                        ALU.mult, ALU.mult))(),
                                    deps=[e3] + sch.wr(wmv_b[wslot_w]) if mm == 0 else [e3],
                                    inc=True)
                        sch.add_reader(t2_b[t2slot], e4)
                        wmv_evs.append(e4)
                        sch.add_reader(bb_, e4)
                    sch.set_ready(wmv_b[wslot_w], wmv_evs[-1])
                    c1 = sch.op("dve", lambda: nc.vector.tensor_mul(
                        kc[:, 56:64], kc[:, 40:48], kc[:, 8:16]),
                        deps=[rden_ev], inc=True)
                    c2 = sch.op("dve", lambda: nc.vector.tensor_mul(
                        kc[:, 56:64], kc[:, 56:64], kc[:, 24:32]), deps=[c1], inc=True)
                    c3 = sch.op("dve", lambda: nc.vector.tensor_scalar(
                        kc[:, 56:64], kc[:, 56:64], 1.0, 0.0, ALU.add, ALU.max),
                        deps=[c2], inc=True)
                    c4 = sch.op("dve", lambda: nc.vector.tensor_scalar_min(
                        kc[:, 56:64], kc[:, 56:64], 1.0), deps=[c3], inc=True)
                    dstc = coefp[:].rearrange("p (m c) -> p m c", c=4)[:, :, q4:q4 + 1]
                    c5 = sch.op("dve", (lambda dstc=dstc: lambda: nc.vector.tensor_mul(
                        dstc, kc[:, 56:64].rearrange("p (m o) -> p m o", o=1),
                        kc[:, 16:24].rearrange("p (m o) -> p m o", o=1)))(),
                        deps=[c4] + sch.wr(coefp_b) if q4 == 0 else [c4], inc=True)
                    sch.add_reader(coefp_b, c5)
                    if q4 == 3:
                        sch.set_ready(coefp_b, c5)
                    dev_ = None
                    dbanks = [pb[2], pb[3]]
                    ddeps = sch.wr(*dbanks) + [wmv_evs[0]] + sch.rd(mkraw_b)
                    for mm in range(8):
                        for fc in range(NFC):
                            bank = p512[2] if fc < 4 else p512[3]
                            off = fc * 128 if fc < 4 else (fc - 4) * 128
                            if mm == 0 and fc == 0:
                                mdeps = ddeps
                            elif fc == 0:
                                mdeps = [wmv_evs[mm]]
                            else:
                                mdeps = ()
                            dev_ = sch.op("pe", (lambda mm=mm, fc=fc, bank=bank, off=off,
                                          ws=wslot_w: lambda: nc.tensor.matmul(
                                              bank[:, off:off + 128],
                                              mkraw[:, mm * FD + fc * 128: mm * FD + (fc + 1) * 128],
                                              wmv_sb[ws][:, mm * 128:(mm + 1) * 128],
                                              start=(mm == 0), stop=(mm == 7)))(),
                                          deps=mdeps,
                                          inc=(mm == 7 and fc == NFC - 1))
                    sch.set_ready(pb[2], dev_)
                    sch.set_ready(pb[3], dev_)
                    sch.add_reader(wmv_b[wslot_w], dev_)
                    dslot = 0
                    ee1 = sch.op("dve", (lambda dslot=dslot: lambda: nc.vector.tensor_copy(
                        dsb[dslot][:, 0:512], p512[2][:]))(),
                        deps=[dev_] + sch.wr(dsb_b[dslot]), inc=True)
                    ee2 = sch.op("dve", (lambda dslot=dslot: lambda: nc.vector.tensor_copy(
                        dsb[dslot][:, 512:768], p512[3][:, 0:256]))(), inc=True)
                    sch.add_reader(pb[2], ee2)
                    sch.add_reader(pb[3], ee2)
                    sch.set_ready(dsb_b[dslot], ee2)
                    buf_idx = qh // 8
                    hh = qh % 8
                    dst = dAB_in[buf_idx][hh:hh + 1, 0:FD * HD].rearrange(
                        "o (f p d) -> p (o f) d", f=NFC, p=128, d=128)
                    dde = sch.dma("pool", (lambda dst=dst, dslot=dslot, th=th: lambda:
                                  nc.gpsimd.dma_start(
                                      out=dst,
                                      in_=dsb[dslot][:].rearrange("p (f d) -> p f d", d=128),
                                      accum_op=(ALU.add if th == 1 else ALU.bypass)))(),
                                  deps=sch.rd(dsb_b[dslot]), sem=f"d_dd{dslot}")
                    sch.add_reader(dsb_b[dslot], dde)
                    delta_dma_evs[buf_idx].append(dde)
                dn_evs = []
                for part in range(2):
                    n0, nw = (0, 512) if part == 0 else (512, 256)
                    dnv = None
                    for mm in range(8):
                        dnv = sch.op("pe", (lambda mm=mm, n0=n0, nw=nw: lambda:
                                     nc.tensor.matmul(
                                         p512[4][0:4, 0:nw],
                                         coefp[:, mm * 4:(mm + 1) * 4],
                                         mkraw[:, mm * FD + n0: mm * FD + n0 + nw],
                                         start=(mm == 0), stop=(mm == 7)))(),
                                     deps=(sch.wr(pb[4]) + sch.rd(coefp_b, mkraw_b))
                                     if mm == 0 else (),
                                     inc=(mm == 7))
                    sch.set_ready(pb[4], dnv)
                    ev = sch.op("dve", (lambda n0=n0, nw=nw: lambda: nc.vector.tensor_copy(
                        dnsb[0:4, n0:n0 + nw], p512[4][0:4, 0:nw]))(),
                        deps=sch.rd(pb[4]) + (sch.wr(dnsb_b) if part == 0 else []),
                        inc=True)
                    sch.add_reader(pb[4], ev)
                    dn_evs.append(ev)
                sch.set_ready(dnsb_b, dn_evs[-1])
                buf_idx = g // 2
                row0 = (g % 2) * 4
                dnd = sch.dma("pool", (lambda buf_idx=buf_idx, row0=row0, th=th: lambda:
                              nc.gpsimd.dma_start(
                                  out=dAB_in[buf_idx][row0:row0 + 4, FD * HD:FD * HD + FD],
                                  in_=dnsb[0:4, :],
                                  accum_op=(ALU.add if th == 1 else ALU.bypass)))(),
                              deps=sch.rd(dnsb_b), sem="d_dn")
                sch.add_reader(dnsb_b, dnd)
                delta_dma_evs[buf_idx].append(dnd)

        ar_evs = [None, None]
        newmem_ready = [None, None]

        def issue_collective(buf_idx):
            ev = sch.op("pool", (lambda b=buf_idx: lambda: nc.gpsimd.collective_compute(
                "AllReduce", ALU.add, replica_groups=[list(range(8))],
                ins=[dAB_in[b].opt()], outs=[dAB_out[b].opt()]))(),
                deps=delta_dma_evs[buf_idx], inc=True, sem="c_ar")
            ar_evs[buf_idx] = ev

        def update_memnorm(buf_idx):
            half = buf_idx
            evs = []
            for hh in range(8):
                h = half * 8 + hh
                mev = sch.dma("sp", (lambda h=h: lambda: nc.sync.dma_start(
                    out=mstg[:].rearrange("p (f d) -> p f d", d=128),
                    in_=mem_hbm[h].rearrange("f p d -> p f d")))(),
                    deps=sch.wr(mstg_b), sem="d_ms")
                sch.set_ready(mstg_b, mev)
                dev_ = sch.dma("sp", (lambda hh=hh, b=buf_idx: lambda: nc.sync.dma_start(
                    out=dstg[:].rearrange("p (f d) -> p f d", d=128),
                    in_=dAB_out[b][hh:hh + 1, 0:FD * HD]
                    .rearrange("o (f p d) -> p (o f) d", f=NFC, p=128, d=128)))(),
                    deps=[ar_evs[buf_idx]] + sch.wr(dstg_b), sem="d_ds")
                sch.set_ready(dstg_b, dev_)
                dst = memnorm[:, h * NFC * MNP:(h + 1) * NFC * MNP]
                dst = dst.rearrange("p (f c) -> p f c", c=MNP)[:, :, 0:128]
                aev = sch.op("dve", (lambda dst=dst: lambda: nc.vector.tensor_add(
                    dst, mstg[:].rearrange("p (f d) -> p f d", d=128),
                    dstg[:].rearrange("p (f d) -> p f d", d=128)))(),
                    deps=[mev, dev_] + (memnorm_b[half].readers if hh == 0 else []),
                    inc=True)
                sch.add_reader(mstg_b, aev)
                sch.add_reader(dstg_b, aev)
                evs.append(aev)
            ndev = sch.dma("sp", (lambda b=buf_idx: lambda: nc.sync.dma_start(
                out=norm16[0:8, :], in_=dAB_out[b][0:8, FD * HD:FD * HD + FD]))(),
                deps=[ar_evs[buf_idx]] + sch.wr(ntt_b), sem="d_dn2")
            for fc in range(NFC):
                tev = sch.op("pe", (lambda fc=fc: lambda: nc.tensor.transpose(
                    p512[5][:, 0:8], norm16[0:8, fc * 128:(fc + 1) * 128],
                    identf[0:8, 0:8]))(),
                    deps=[ndev] + sch.wr(pb[5]), inc=True)
                sch.set_ready(pb[5], tev)
                dst = memnorm[:, half * 8 * NFC * MNP:(half * 8 + 8) * NFC * MNP]
                dst = dst.rearrange("p (h c) -> p h c", c=NFC * MNP)[
                    :, :, fc * MNP + 128: fc * MNP + 129]
                ev = sch.op("dve", (lambda dst=dst, fc=fc, half=half: lambda:
                            nc.vector.tensor_add(
                                dst, norm_tt[:, fc * 16 + half * 8: fc * 16 + half * 8 + 8]
                                .rearrange("p (h o) -> p h o", o=1),
                                p512[5][:, 0:8].rearrange("p (h o) -> p h o", o=1)))(),
                    deps=sch.rd(pb[5]), inc=True)
                sch.add_reader(pb[5], ev)
                evs.append(ev)
            nmev = [Ev(("s_dve", sch.cnt["s_dve"]))]
            memnorm_b[half].ready = None
            memnorm_b[half].readers = []
            newmem_ready[half] = evs[-1:]
            newmem_ready[half] = nmev

        kside_group(0)
        kside_group(1)
        issue_collective(0)
        kside_group(2)
        kside_group(3)
        issue_collective(1)
        update_memnorm(0)

        def qside_head(qh):
            half = qh // 8
            if qh == 8:
                update_memnorm(1)
            ws = qh % 2
            wq_ev = sch.dma("pool", (lambda qh=qh, ws=ws: lambda: nc.gpsimd.dma_start(
                out=(wkg if ws == 0 else wvg)[:].rearrange("p (c o) -> p c o", o=128),
                in_=wq_hbm[:, :, qh * 128:(qh + 1) * 128].rearrange("c p o -> p c o")))(),
                deps=sch.wr(wkg_b if ws == 0 else wvg_b), sem=f"d_wq{ws}")
            wq_buf = wkg_b if ws == 0 else wvg_b
            wq_t = wkg if ws == 0 else wvg
            sch.set_ready(wq_buf, wq_ev)
            wg_ev = sch.dma("pool", (lambda qh=qh, ws=ws: lambda: nc.gpsimd.dma_start(
                out=wmbg[ws][:].rearrange("p (c o) -> p c o", o=128),
                in_=wg_hbm[:, :, qh * 128:(qh + 1) * 128].rearrange("c p o -> p c o")))(),
                deps=sch.wr(wmbg_b[ws]), sem=f"d_wg{ws}")
            sch.set_ready(wmbg_b[ws], wg_ev)
            xs = qh % 2
            xe1 = sch.dma("sp", (lambda qh=qh, xs=xs: lambda: nc.sync.dma_start(
                out=xt[xs][:, 0:1024].rearrange("p (m c) -> p m c", c=128),
                in_=x_hbm[0:1024, qh * 128:(qh + 1) * 128]
                .rearrange("(m p) c -> p m c", p=128)))(),
                deps=sch.wr(xt2_b[xs]), sem=f"d_xc{xs}")
            xe2 = sch.dma("sp", (lambda qh=qh, xs=xs: lambda: nc.sync.dma_start(
                out=xt[xs][:, 1024:2048].rearrange("p (m c) -> p m c", c=128),
                in_=x_hbm[1024:2048, qh * 128:(qh + 1) * 128]
                .rearrange("(m p) c -> p m c", p=128)))(), sem=f"d_xc{xs}")
            sch.set_ready(xt2_b[xs], xe2)
            xcol_ready = [xe1, xe2]
            for th in range(2):
                qevs = proj8(th, (lambda w=wq_t: lambda ci: w[:, ci * 128:(ci + 1) * 128])(),
                             BQ0 + qh * 128, evac_relu_pair(r_sb, r_b), wq_buf)
                sch.set_ready(r_b, qevs[-1])
                fm_ready, (nev, rnev, rqev) = dpfp8(qevs)
                gevs = proj8(th, (lambda ws=ws: lambda ci: wmbg[ws][:, ci * 128:(ci + 1) * 128])(),
                             BG0 + qh * 128, evac_copy(g_sb[th], g_b[th], AF.Sigmoid),
                             wmbg_b[ws])
                sch.set_ready(g_b[th], gevs[-1])
                bank_evs, rden_ev = numden(qh, newmem_ready[half], fm_ready)
                a_evs = []
                for mm in range(8):
                    m = th * 8 + mm
                    pair = mm // 2
                    j = mm % 2
                    bank, bb_ = p512[2 + pair], pb[2 + pair]
                    num = bank[:, j * MNP:j * MNP + 128]
                    xcslot = mm % 2
                    xce = sch.op("dve", (lambda m=m, xs=xs, xcslot=xcslot: lambda:
                                 nc.vector.tensor_scalar(
                                     xc_sb[:, xcslot * 128:(xcslot + 1) * 128],
                                     xt[xs][:, m * 128:(m + 1) * 128],
                                     m1_sb[:, m:m + 1], r1_sb[:, m:m + 1],
                                     ALU.subtract, ALU.mult))(),
                                 deps=xcol_ready + sch.wr(xc_b[xcslot]), inc=True)
                    sch.set_ready(xc_b[xcslot], xce)
                    sch.add_reader(xt2_b[xs], xce)
                    t2slot = mm % 2
                    e3 = sch.op("dve", (lambda num=num, mm=mm, th=th, t2slot=t2slot: lambda:
                                nc.vector.scalar_tensor_tensor(
                                    t2_sb[:, t2slot * 128:(t2slot + 1) * 128],
                                    num, kc[:, 48 + mm:49 + mm],
                                    g_sb[th][:, mm * 128:(mm + 1) * 128],
                                    ALU.mult, ALU.mult))(),
                                deps=[rden_ev] + sch.rd(g_b[th]) + sch.wr(t2_b[t2slot]),
                                inc=True)
                    sch.set_ready(t2_b[t2slot], e3)
                    sch.add_reader(bb_, e3)
                    aslot = 0
                    e4 = sch.op("dve", (lambda mm=mm, t2slot=t2slot, xcslot=xcslot,
                                aslot=aslot: lambda: nc.vector.scalar_tensor_tensor(
                                    aout_sb[aslot][:, mm * 128:(mm + 1) * 128],
                                    t2_sb[:, t2slot * 128:(t2slot + 1) * 128], -1.0,
                                    xc_sb[:, xcslot * 128:(xcslot + 1) * 128],
                                    ALU.mult, ALU.add,
                                    accum_out=cols[:, 8:9]))(),
                                deps=([e3, xce] + sch.wr(aout_b[0])) if mm == 0
                                else [e3, xce], inc=True)
                    sch.add_reader(t2_b[t2slot], e4)
                    sch.add_reader(xc_b[xcslot], e4)
                    e5 = sch.op("dve", (lambda m=m: lambda: nc.vector.tensor_add(
                        s2_sb[:, m:m + 1], s2_sb[:, m:m + 1], cols[:, 8:9]))(),
                        deps=[e4], inc=True)
                    e6 = sch.op("act", (lambda mm=mm, aslot=aslot: lambda:
                                nc.scalar.activation(
                                    sqdump[:, 0:128],
                                    aout_sb[aslot][:, mm * 128:(mm + 1) * 128],
                                    AF.Square, accum_out=cols[:, 9:10]))(),
                                deps=[e4], inc=True)
                    e7 = sch.op("dve", (lambda m=m: lambda: nc.vector.tensor_add(
                        q2_sb[:, m:m + 1], q2_sb[:, m:m + 1], cols[:, 9:10]))(),
                        deps=[e6], inc=True)
                    a_evs.append(e4)
                    last_sq = e6
                sch.set_ready(aout_b[0], a_evs[-1])
                sch.add_reader(aout_b[0], last_sq)
                dst = aout_dram[th * 1024:(th + 1) * 1024, qh * 128:(qh + 1) * 128]
                dst = dst.rearrange("(m p) c -> p m c", p=128)
                ade = sch.dma("sp", (lambda dst=dst, th=th: lambda: nc.sync.dma_start(
                    out=dst, in_=aout_sb[0][:].rearrange("p (m c) -> p m c", c=128)))(),
                    deps=sch.rd(aout_b[0]), sem=f"d_ao{th}")
                sch.add_reader(aout_b[0], ade)

        for qh in range(HQ):
            qside_head(qh)

        f1 = sch.op("dve", lambda: nc.vector.tensor_scalar_mul(
            m2_sb[:], s2_sb[:], 1.0 / DM), deps=sch.snapshot(), inc=True)
        f2 = sch.op("dve", lambda: nc.vector.tensor_scalar_mul(
            q2_sb[:], q2_sb[:], 1.0 / DM), deps=[f1], inc=True)
        f3 = sch.op("dve", lambda: nc.vector.tensor_mul(
            s2_sb[:], m2_sb[:], m2_sb[:]), deps=[f2], inc=True)
        f4 = sch.op("dve", lambda: nc.vector.tensor_sub(
            q2_sb[:], q2_sb[:], s2_sb[:]), deps=[f3], inc=True)
        f5 = sch.op("act", lambda: nc.scalar.activation(
            q2_sb[:], q2_sb[:], AF.Sqrt, bias=EPS_LN), deps=[f4], inc=True)
        f6 = sch.op("dve", lambda: nc.vector.reciprocal(r2_sb[:], q2_sb[:]),
                    deps=[f5], inc=True)
        ln2_ready = f6

        if "dAin" in taps:
            sch.dma("pool", lambda: nc.gpsimd.dma_start(out=taps["dAin"], in_=dAB_in[0]),
                    deps=delta_dma_evs[0], sem="d_tap")
        if "dAout" in taps:
            sch.dma("pool", lambda: nc.gpsimd.dma_start(out=taps["dAout"], in_=dAB_out[0]),
                    deps=[ar_evs[0]], sem="d_tap")

        sch.set_floor()
        at_ctx.close()

        ffn_ctx = ExitStack()
        x2h_fm = sb("x2h_fm", [128, 16 * 1024], BF16, ffn_ctx)
        x2h_b = Buf()
        p2_ctx = ExitStack()
        xrow = [sb(f"xrow{i}", [128, DM], F32, p2_ctx) for i in range(2)]
        xrow_b = [Buf(), Buf()]
        arow = [sb(f"arow{i}", [128, DM], F32, p2_ctx) for i in range(2)]
        arow_b = [Buf(), Buf()]
        x1t = [sb(f"x1t{i}", [128, DM], F32, p2_ctx) for i in range(2)]
        x1t_b = [Buf(), Buf()]
        x2bf = [sb(f"x2bf{i}", [128, DM], BF16, p2_ctx) for i in range(2)]
        x2bf_b = [Buf(), Buf()]

        x2h_evs = []
        for m in list(range(8, NT)) + list(range(8)):
            slot = m % 2
            xb = xrow_b[slot]
            ld = sch.dma("sp", (lambda mm=m, ss=slot: lambda: nc.sync.dma_start(
                out=xrow[ss][:], in_=x_hbm[mm * 128:(mm + 1) * 128, :]))(),
                deps=sch.wr(xb), sem=f"d_x{slot}")
            sch.set_ready(xb, ld)
            ab = arow_b[slot]
            ald = sch.dma("sp", (lambda mm=m, ss=slot: lambda: nc.sync.dma_start(
                out=arow[ss][:], in_=aout_dram[mm * 128:(mm + 1) * 128, :]))(),
                deps=sch.wr(ab), sem=f"d_ar{slot}")
            sch.set_ready(ab, ald)
            e1 = sch.op("dve", (lambda ss=slot, m=m: lambda: nc.vector.tensor_scalar(
                arow[ss][:], arow[ss][:], m2_sb[:, m:m + 1], r2_sb[:, m:m + 1],
                ALU.subtract, ALU.mult))(),
                deps=sch.rd(ab) + [ln2_ready], inc=True)
            xb1 = x1t_b[slot]
            e2 = sch.op("dve", (lambda ss=slot: lambda: nc.vector.tensor_add(
                x1t[ss][:], xrow[ss][:], arow[ss][:]))(),
                deps=[e1] + sch.rd(xb) + sch.wr(xb1), inc=True)
            sch.add_reader(xb, e2)
            sch.add_reader(ab, e2)
            sch.set_ready(xb1, e2)
            x1de = sch.dma("sp", (lambda mm=m, ss=slot: lambda: nc.sync.dma_start(
                out=x1_dram[mm * 128:(mm + 1) * 128, :], in_=x1t[ss][:]))(),
                deps=sch.rd(xb1), sem=f"d_x1{slot}")
            sch.add_reader(xb1, x1de)
            bb2 = x2bf_b[slot]
            e7 = ln_stats_and_scale(x1t[slot][:], arow[slot][:],
                                    m3_sb[:, m:m + 1], r3_sb[:, m:m + 1],
                                    (m % 2) * 4,
                                    [x1de] + sch.wr(bb2) + sch.rd(xb1), x2bf[slot][:])
            sch.set_ready(bb2, e7)
            sch.add_reader(xb1, e7)
            sch.add_reader(ab, e7)
            if m < 8:
                srcs = [x2bf[slot][:, ci * 128:(ci + 1) * 128] for ci in range(16)]
                dsts = [x2h_fm[:, ci * 1024 + m * 128: ci * 1024 + (m + 1) * 128]
                        for ci in range(16)]
                evs = transpose_tiles(srcs, dsts, sch.rd(bb2))
                for e in evs:
                    sch.add_reader(bb2, e)
                x2h_evs += evs[-1:]
                if m == 7:
                    x2h_ready = [Ev(("s_act", sch.cnt["s_act"])),
                                 Ev(("s_dve", sch.cnt["s_dve"]))]
            else:
                de = sch.dma("sp", (lambda mm=m, ss=slot: lambda: nc.sync.dma_start(
                    out=x2b_dram[mm - 8], in_=x2bf[ss][:]))(),
                    deps=sch.rd(bb2), sem=f"d_x2b{slot}")
                sch.add_reader(bb2, de)

        if "x1d" in taps:
            sch.dma("pool", lambda: nc.gpsimd.dma_start(out=taps["x1d"], in_=x1_dram),
                    deps=sch.snapshot(), sem="d_tap")
        if "aoutd" in taps:
            sch.dma("pool", lambda: nc.gpsimd.dma_start(out=taps["aoutd"], in_=aout_dram),
                    deps=sch.snapshot(), sem="d_tap")

        sch.set_floor()
        p2_ctx.close()

        acc = sb("acc", [128, 16 * 1024], F32, ffn_ctx)
        h_sb = sb("h_sb", [128, 11 * 1024], BF16, ffn_ctx)
        wblk = [sb(f"wblk{i}", [128, 2048], BF16, ffn_ctx) for i in range(4)]
        w2blk = [sb(f"w2blk{i}", [128, 11 * 128], BF16, ffn_ctx) for i in range(2)]
        sdwblk = w2blk
        gw_sb = sb("gw_sb", [128, 16 * NE], BF16, ffn_ctx)
        sgw_sb = sb("sgw_sb", [128, 16], BF16, ffn_ctx)
        lg_sb = sb("lg_sb", [128, 8 * NE], F32, ffn_ctx)
        wrk_sb = sb("wrk_sb", [128, 8 * NE], F32, ffn_ctx)
        coef_tm = sb("coef_tm", [128, 8 * NE], F32, ffn_ctx)
        coef_fm = sb("coef_fm", [16, 1024], F32, ffn_ctx)
        sg_row = sb("sg_row", [1, 1024], F32, ffn_ctx)
        ones_f32 = sb("ones_f32", [1, 128], F32, ffn_ctx)
        onehot_sb = sb("onehot", [16, 128], F32, ffn_ctx)
        cr_sb = sb("cr_sb", [128, 1024], F32, ffn_ctx)
        sgr_sb = sb("sgr_sb", [128, 1024], F32, ffn_ctx)
        gs_sb = sb("gs_sb", [128, 1024], BF16, ffn_ctx)
        us_sb = sb("us_sb", [128, 1024], BF16, ffn_ctx)
        acct = [sb(f"acct{i}", [128, DM], F32, ffn_ctx) for i in range(2)]
        x1f = [sb(f"x1f{i}", [128, DM], F32, ffn_ctx) for i in range(2)]
        xrb = [sb(f"xrb{i}", [128, DM], BF16, ffn_ctx) for i in range(2)]
        fdump = xrb[1]
        fcols = sb("fcols", [128, 16], F32, ffn_ctx)

        acc_b = Buf()
        h_b = Buf()
        wblk_b = [Buf() for _ in wblk]
        w2blk_b = [Buf() for _ in w2blk]
        sdwblk_b = w2blk_b
        gw_b, sgw_b = Buf(), Buf()
        lg_b, cfm_b = Buf(), Buf()
        sgrow_b, cr_b, sgr_b = Buf(), Buf(), Buf()
        gs_b, us_b = Buf(), Buf()
        acct_b = [Buf(), Buf()]
        x1f_b = [Buf(), Buf()]
        xrb_b = [Buf(), Buf()]

        sch.op("pool", lambda: nc.gpsimd.memset(ones_f32[:], 1.0))
        onehot_b = Buf()

        def set_onehot(e):
            mz = sch.op("pool", lambda: nc.gpsimd.memset(onehot_sb[:], 0.0),
                        deps=sch.wr(onehot_b), inc=True)
            ev = sch.op("pool", (lambda e=e: lambda: nc.gpsimd.affine_select(
                out=onehot_sb[:], in_=onehot_sb[:], compare_op=ALU.not_equal,
                fill=1.0, base=-e, pattern=[[0, 128]], channel_multiplier=1))(),
                deps=[mz], inc=True)
            sch.set_ready(onehot_b, ev)
        gwe = sch.dma("pool", lambda: nc.gpsimd.dma_start(
            out=gw_sb[:].rearrange("p (c o) -> p c o", o=NE),
            in_=gatew_hbm.rearrange("c p o -> p c o")), sem="d_gw")
        sch.set_ready(gw_b, gwe)
        sgwe = sch.dma("pool", lambda: nc.gpsimd.dma_start(
            out=sgw_sb[:], in_=sgatew_hbm), sem="d_gw2")
        sch.set_ready(sgw_b, sgwe)

        def x2_slice(ci, tg):
            return x2h_fm[:, ci * 1024 + tg * 512: ci * 1024 + (tg + 1) * 512]

        def router(hf, x2_ready):
            bank, bb_ = p512[0], pb[0]
            mev = None
            deps = sch.wr(bb_) + list(x2_ready) + sch.rd(gw_b)
            for m8 in range(8):
                for ci in range(16):
                    mev = sch.op("pe", (lambda m8=m8, ci=ci, bank=bank: lambda:
                                 nc.tensor.matmul(
                                     bank[:, m8 * 16:(m8 + 1) * 16],
                                     x2h_fm[:, ci * 1024 + m8 * 128: ci * 1024 + (m8 + 1) * 128],
                                     gw_sb[:, ci * 16:(ci + 1) * 16],
                                     start=(ci == 0), stop=(ci == 15)))(),
                                 deps=deps if (m8 == 0 and ci == 0) else (),
                                 inc=(m8 == 7 and ci == 15))
            sch.set_ready(bb_, mev)
            sch.add_reader(gw_b, mev)
            lev = sch.op("dve", lambda: nc.vector.tensor_copy(lg_sb[:], p512[0][:, 0:128]),
                         deps=sch.rd(bb_) + sch.wr(lg_b), inc=True)
            sch.add_reader(bb_, lev)
            sch.set_ready(lg_b, lev)
            cevs = []
            for m8 in range(8):
                s0, s1 = m8 * 16, (m8 + 1) * 16
                e1 = sch.op("dve", (lambda s0=s0, s1=s1: lambda: nc.vector.reduce_max(
                    fcols[:, 0:1], lg_sb[:, s0:s1], axis=AX.X, negate=True))(),
                    deps=[lev] if m8 == 0 else [], inc=True)
                e2 = sch.op("act", (lambda s0=s0, s1=s1: lambda: nc.scalar.activation(
                    wrk_sb[:, s0:s1], lg_sb[:, s0:s1], AF.Exp, bias=fcols[:, 0:1]))(),
                    deps=[e1], inc=True)
                e3 = sch.op("dve", (lambda s0=s0, s1=s1: lambda: nc.vector.reduce_sum(
                    fcols[:, 1:2], wrk_sb[:, s0:s1], axis=AX.X))(), deps=[e2], inc=True)
                e4 = sch.op("dve", lambda: nc.vector.reciprocal(
                    fcols[:, 2:3], fcols[:, 1:2]), deps=[e3], inc=True)
                e5 = sch.op("dve", (lambda s0=s0, s1=s1: lambda:
                            nc.vector.tensor_single_scalar(
                                wrk_sb[:, s0:s1], wrk_sb[:, s0:s1], fcols[:, 2:3],
                                ALU.mult))(), deps=[e4], inc=True)
                prev = sch.op("dve", (lambda s0=s0, s1=s1: lambda: nc.vector.tensor_copy(
                    coef_tm[:, s0:s1], wrk_sb[:, s0:s1]))(), deps=[e5], inc=True)
                for r_ in range(4):
                    mx = sch.op("dve", (lambda s0=s0, s1=s1: lambda: nc.vector.reduce_max(
                        fcols[:, 3:4], coef_tm[:, s0:s1], axis=AX.X))(),
                        deps=[prev], inc=True)
                    if r_ < 3:
                        mk_ = sch.op("dve", (lambda s0=s0, s1=s1: lambda:
                                     nc.vector.tensor_single_scalar(
                                         lg_sb[:, s0:s1], coef_tm[:, s0:s1],
                                         fcols[:, 3:4], ALU.is_equal))(),
                                     deps=[mx], inc=True)
                        prev = sch.op("dve", (lambda s0=s0, s1=s1: lambda:
                                      nc.vector.scalar_tensor_tensor(
                                          coef_tm[:, s0:s1], lg_sb[:, s0:s1], -2.0,
                                          coef_tm[:, s0:s1], ALU.mult, ALU.add))(),
                                      deps=[mk_], inc=True)
                    else:
                        prev = mx
                e8 = sch.op("dve", (lambda s0=s0, s1=s1: lambda:
                            nc.vector.tensor_single_scalar(
                                coef_tm[:, s0:s1], wrk_sb[:, s0:s1], fcols[:, 3:4],
                                ALU.is_ge))(), deps=[prev], inc=True)
                e9 = sch.op("dve", (lambda s0=s0, s1=s1: lambda: nc.vector.tensor_mul(
                    coef_tm[:, s0:s1], coef_tm[:, s0:s1], wrk_sb[:, s0:s1]))(),
                    deps=[e8], inc=True)
                cevs.append(e9)
            fev = None
            for m8 in range(8):
                tev = sch.op("pe", (lambda m8=m8: lambda: nc.tensor.transpose(
                    p512[1][0:16, 0:128], coef_tm[:, m8 * 16:(m8 + 1) * 16],
                    identf[:]))(),
                    deps=[cevs[m8]] + sch.wr(pb[1]), inc=True)
                sch.set_ready(pb[1], tev)
                fev = sch.op("dve", (lambda m8=m8: lambda: nc.vector.tensor_copy(
                    coef_fm[0:16, m8 * 128:(m8 + 1) * 128], p512[1][0:16, 0:128]))(),
                    deps=sch.rd(pb[1]) + (sch.wr(cfm_b) if m8 == 0 else []), inc=True)
                sch.add_reader(pb[1], fev)
            sch.set_ready(cfm_b, fev)
            for tg in range(2):
                bank, bb_ = p512[1], pb[1]
                sgev = None
                for ci in range(16):
                    sgev = sch.op("pe", (lambda ci=ci, tg=tg, bank=bank: lambda:
                                  nc.tensor.matmul(
                                      bank[0:1, 0:512],
                                      sgw_sb[:, ci:ci + 1], x2_slice(ci, tg),
                                      start=(ci == 0), stop=(ci == 15)))(),
                                  deps=(sch.wr(bb_) + sch.rd(sgw_b) + list(x2_ready))
                                  if ci == 0 else (), inc=(ci == 15))
                sch.set_ready(bb_, sgev)
                ev = sch.op("act", (lambda tg=tg: lambda: nc.scalar.activation(
                    sg_row[0:1, tg * 512:(tg + 1) * 512], p512[1][0:1, 0:512],
                    AF.Sigmoid))(),
                    deps=sch.rd(pb[1]) + (sch.wr(sgrow_b) if tg == 0 else []), inc=True)
                sch.add_reader(pb[1], ev)
            sch.set_ready(sgrow_b, ev)
            for tg in range(2):
                bank, bb_ = p512[1], pb[1]
                bev = sch.op("pe", (lambda tg=tg, bank=bank: lambda: nc.tensor.matmul(
                    bank[:], ones_f32[0:1, :], sg_row[0:1, tg * 512:(tg + 1) * 512],
                    start=True, stop=True))(),
                    deps=sch.wr(bb_) + sch.rd(sgrow_b), inc=True)
                sch.set_ready(bb_, bev)
                ev = sch.op("dve", (lambda tg=tg: lambda: nc.vector.tensor_copy(
                    sgr_sb[:, tg * 512:(tg + 1) * 512], p512[1][:]))(),
                    deps=sch.rd(pb[1]) + (sch.wr(sgr_b) if tg == 0 else []), inc=True)
                sch.add_reader(pb[1], ev)
            sch.set_ready(sgr_b, ev)

        def gu_pair(ip, s0, s1, x2_ready, rep_b, rep_sb):
            hev = None
            for tg in range(2):
                gbank, gb_ = p512[2 * tg], pb[2 * tg]
                ubank, ub_ = p512[2 * tg + 1], pb[2 * tg + 1]
                mg = None
                for ci in range(16):
                    mg = sch.op("pe", (lambda ci=ci, tg=tg, gbank=gbank, s0=s0: lambda:
                                nc.tensor.matmul(
                                    gbank[:], wblk[s0][:, ci * 128:(ci + 1) * 128],
                                    x2_slice(ci, tg), start=(ci == 0), stop=(ci == 15)))(),
                                deps=(sch.wr(gb_) + sch.rd(wblk_b[s0]) + list(x2_ready))
                                if ci == 0 else (), inc=(ci == 15))
                sch.set_ready(gb_, mg)
                sch.add_reader(wblk_b[s0], mg)
                mu = None
                for ci in range(16):
                    mu = sch.op("pe", (lambda ci=ci, tg=tg, ubank=ubank, s1=s1: lambda:
                                nc.tensor.matmul(
                                    ubank[:], wblk[s1][:, ci * 128:(ci + 1) * 128],
                                    x2_slice(ci, tg), start=(ci == 0), stop=(ci == 15)))(),
                                deps=(sch.wr(ub_) + sch.rd(wblk_b[s1]))
                                if ci == 0 else (), inc=(ci == 15))
                sch.set_ready(ub_, mu)
                sch.add_reader(wblk_b[s1], mu)
                ge = sch.op("act", (lambda tg=tg, gbank=gbank: lambda:
                            nc.scalar.activation(
                                gs_sb[:, tg * 512:(tg + 1) * 512], gbank[:], AF.Silu))(),
                            deps=sch.rd(gb_) + (sch.wr(gs_b) if tg == 0 else []),
                            inc=True)
                sch.add_reader(gb_, ge)
                ue = sch.op("dve", (lambda tg=tg, ubank=ubank, rep_sb=rep_sb: lambda:
                            nc.vector.tensor_mul(
                                us_sb[:, tg * 512:(tg + 1) * 512], ubank[:],
                                rep_sb[:, tg * 512:(tg + 1) * 512]))(),
                            deps=sch.rd(ub_, rep_b) + (sch.wr(us_b) if tg == 0 else []),
                            inc=True)
                sch.add_reader(ub_, ue)
                hev = sch.op("dve", (lambda ip=ip, tg=tg: lambda: nc.vector.tensor_mul(
                    h_sb[:, ip * 1024 + tg * 512: ip * 1024 + (tg + 1) * 512],
                    gs_sb[:, tg * 512:(tg + 1) * 512],
                    us_sb[:, tg * 512:(tg + 1) * 512]))(),
                    deps=[ge, ue] + (sch.wr(h_b) if (ip == 0 and tg == 0) else []),
                    inc=True)
            sch.add_reader(gs_b, hev)
            sch.add_reader(us_b, hev)
            return hev

        acc_last = [None] * 32

        def down_16(wtensor, wbufs, load_fn, h_ready, first_flags):
            for o in range(16):
                s = o % 2
                lw = sch.dma("pool", (lambda o=o, s=s: lambda: load_fn(o, s))(),
                             deps=sch.wr(wbufs[s]), sem=f"d_dw{s}")
                sch.set_ready(wbufs[s], lw)
                for tg in range(2):
                    bank, bb_ = p512[4 + tg], pb[4 + tg]
                    mev = None
                    for ip in range(11):
                        mev = sch.op("pe", (lambda o=o, ip=ip, tg=tg, bank=bank,
                                     wt=wtensor, s=s: lambda: nc.tensor.matmul(
                                         bank[:], wt[s][:, ip * 128:(ip + 1) * 128],
                                         h_sb[:, ip * 1024 + tg * 512: ip * 1024 + (tg + 1) * 512],
                                         start=(ip == 0), stop=(ip == 10)))(),
                                     deps=(sch.wr(bb_) + [lw] + list(h_ready))
                                     if ip == 0 else (), inc=(ip == 10))
                    sch.set_ready(bb_, mev)
                    sch.add_reader(wbufs[s], mev)
                    sch.add_reader(h_b, mev)
                    ot = o * 1024 + tg * 512
                    idx = o * 2 + tg
                    chain = [acc_last[idx]] if acc_last[idx] is not None else []
                    if first_flags[idx]:
                        first_flags[idx] = False
                        war = acc_b.readers + chain
                        acc_b.readers = []
                        ev = sch.op("dve", (lambda ot=ot, bank=bank: lambda:
                                    nc.vector.tensor_copy(acc[:, ot:ot + 512], bank[:]))(),
                                    deps=sch.rd(bb_) + war, inc=True)
                    else:
                        ev = sch.op("dve", (lambda ot=ot, bank=bank: lambda:
                                    nc.vector.tensor_add(
                                        acc[:, ot:ot + 512], acc[:, ot:ot + 512],
                                        bank[:]))(),
                                    deps=sch.rd(bb_) + chain, inc=True)
                    acc_last[idx] = ev
                    sch.add_reader(bb_, ev)

        def ffn_half(hf, x2_ready, mid_cb=None):
            x2h_free = [None]
            router(hf, x2_ready)
            first_flags = [True] * 32
            for ig in range(4):
                for ip in range(11):
                    i = ig * 11 + ip
                    s0, s1 = 2 * (ip % 2), 2 * (ip % 2) + 1
                    le0 = sch.dma("pool", (lambda i=i, s0=s0: lambda: nc.gpsimd.dma_start(
                        out=wblk[s0][:], in_=sgu_hbm[i].rearrange("p c o -> p (c o)")))(),
                        deps=sch.wr(wblk_b[s0]), sem=f"d_wb{s0}")
                    sch.set_ready(wblk_b[s0], le0)
                    le1 = sch.dma("pool", (lambda i=i, s1=s1: lambda: nc.gpsimd.dma_start(
                        out=wblk[s1][:], in_=sgu_hbm[i + 44].rearrange("p c o -> p (c o)")))(),
                        deps=sch.wr(wblk_b[s1]), sem=f"d_wb{s1}")
                    sch.set_ready(wblk_b[s1], le1)
                    gu_pair(ip, s0, s1, x2_ready, sgr_b, sgr_sb)
                h_ready = [Ev(("s_dve", sch.cnt["s_dve"]))]

                def sdw_load(o, s, ig=ig):
                    return nc.gpsimd.dma_start(
                        out=sdwblk[s][:],
                        in_=sdw_hbm[o][:, ig * 11:(ig + 1) * 11, :]
                        .rearrange("p c o2 -> p (c o2)"))
                down_16(sdwblk, sdwblk_b, sdw_load, h_ready, first_flags)
            for e in range(NE):
                set_onehot(e)
                crev = None
                for tg in range(2):
                    bank, bb_ = p512[1], pb[1]
                    bev = sch.op("pe", (lambda tg=tg, bank=bank: lambda:
                                 nc.tensor.matmul(
                                     bank[:], onehot_sb[:],
                                     coef_fm[0:16, tg * 512:(tg + 1) * 512],
                                     start=True, stop=True))(),
                                 deps=sch.wr(bb_) + sch.rd(cfm_b, onehot_b), inc=True)
                    sch.add_reader(onehot_b, bev)
                    sch.set_ready(bb_, bev)
                    crev = sch.op("dve", (lambda tg=tg: lambda: nc.vector.tensor_copy(
                        cr_sb[:, tg * 512:(tg + 1) * 512], p512[1][:]))(),
                        deps=sch.rd(pb[1]) + (sch.wr(cr_b) if tg == 0 else []), inc=True)
                    sch.add_reader(pb[1], crev)
                sch.set_ready(cr_b, crev)
                for op_ in range(11):
                    s0, s1 = 2 * (op_ % 2), 2 * (op_ % 2) + 1
                    le0 = sch.dma("pool", (lambda e=e, op_=op_, s0=s0: lambda:
                                  nc.gpsimd.dma_start(
                                      out=wblk[s0][:],
                                      in_=w13_hbm[e, op_].rearrange("p c o -> p (c o)")))(),
                                  deps=sch.wr(wblk_b[s0]), sem=f"d_wb{s0}")
                    sch.set_ready(wblk_b[s0], le0)
                    le1 = sch.dma("pool", (lambda e=e, op_=op_, s1=s1: lambda:
                                  nc.gpsimd.dma_start(
                                      out=wblk[s1][:],
                                      in_=w13_hbm[e, op_ + 11].rearrange("p c o -> p (c o)")))(),
                                  deps=sch.wr(wblk_b[s1]), sem=f"d_wb{s1}")
                    sch.set_ready(wblk_b[s1], le1)
                    gu_pair(op_, s0, s1, x2_ready, cr_b, cr_sb)
                h_ready = [Ev(("s_dve", sch.cnt["s_dve"]))]
                if e == NE - 1:
                    x2h_free[0] = Ev(("s_pe", sch.cnt["s_pe"]))

                def w2_load(o, s, e=e):
                    return nc.gpsimd.dma_start(
                        out=w2blk[s][:], in_=w2_hbm[e, o].rearrange("p c o2 -> p (c o2)"))
                down_16(w2blk, w2blk_b, w2_load, h_ready, first_flags)
            acc_ready = Ev(("s_dve", sch.cnt["s_dve"]))
            if mid_cb is not None:
                mid_cb(x2h_free[0])
            for mt in range(8):
                m = hf * 8 + mt
                aslot = mt % 2
                srcs = [acc[:, o * 1024 + mt * 128: o * 1024 + (mt + 1) * 128]
                        for o in range(16)]
                dsts = [acct[aslot][:, o * 128:(o + 1) * 128] for o in range(16)]
                accdeps = [acc_ready] + [e_ for e_ in acc_last if e_ is not None]
                tevs = transpose_tiles(srcs, dsts,
                                       accdeps + sch.wr(acct_b[aslot]), dtype="f32")
                for e_ in tevs:
                    sch.add_reader(acc_b, e_)
                sch.set_ready(acct_b[aslot], tevs[-1])
                tready = [Ev(("s_act", sch.cnt["s_act"])), Ev(("s_dve", sch.cnt["s_dve"]))]
                xslot = mt % 2
                x1l = sch.dma("sp", (lambda m=m, ss=xslot: lambda: nc.sync.dma_start(
                    out=x1f[ss][:], in_=x1_dram[m * 128:(m + 1) * 128, :]))(),
                    deps=sch.wr(x1f_b[xslot]), sem=f"d_xf{xslot}")
                sch.set_ready(x1f_b[xslot], x1l)
                c0 = 8 + (mt % 2) * 4
                e1 = sch.op("dve", (lambda aslot=aslot, c0=c0: lambda:
                            nc.vector.reduce_sum(
                                fcols[:, c0:c0 + 1], acct[aslot][:], axis=AX.X))(),
                            deps=tready, inc=True)
                e2 = sch.op("dve", (lambda c0=c0: lambda: nc.vector.tensor_scalar_mul(
                    fcols[:, c0 + 1:c0 + 2], fcols[:, c0:c0 + 1], 1.0 / DM))(),
                    deps=[e1], inc=True)
                e3 = sch.op("dve", (lambda aslot=aslot, c0=c0: lambda:
                            nc.vector.tensor_single_scalar(
                                acct[aslot][:], acct[aslot][:],
                                fcols[:, c0 + 1:c0 + 2], ALU.subtract))(),
                            deps=[e2], inc=True)
                e4 = sch.op("act", (lambda aslot=aslot, c0=c0: lambda:
                            nc.scalar.activation(
                                fdump[:], acct[aslot][:], AF.Square,
                                accum_out=fcols[:, c0 + 2:c0 + 3]))(),
                            deps=[e3], inc=True)
                e5 = sch.op("act", (lambda c0=c0: lambda: nc.scalar.activation(
                    fcols[:, c0 + 3:c0 + 4], fcols[:, c0 + 2:c0 + 3], AF.Sqrt,
                    bias=EPS_LN, scale=1.0 / DM))(), deps=[e4], inc=True)
                e6 = sch.op("dve", (lambda c0=c0: lambda: nc.vector.reciprocal(
                    fcols[:, c0 + 2:c0 + 3], fcols[:, c0 + 3:c0 + 4]))(),
                    deps=[e5], inc=True)
                e7 = sch.op("dve", (lambda aslot=aslot, xslot=xslot, c0=c0: lambda:
                            nc.vector.scalar_tensor_tensor(
                                x1f[xslot][:], acct[aslot][:],
                                fcols[:, c0 + 2:c0 + 3], x1f[xslot][:],
                                ALU.mult, ALU.add))(),
                            deps=[e6] + sch.rd(x1f_b[xslot]), inc=True)
                sch.add_reader(acct_b[aslot], e7)
                sch.set_ready(x1f_b[xslot], e7)
                ode = sch.dma("sp", (lambda m=m, ss=xslot: lambda: nc.sync.dma_start(
                    out=out_hbm[m * 128:(m + 1) * 128, :], in_=x1f[ss][:]))(),
                    deps=sch.rd(x1f_b[xslot]), sem="d_out")
                sch.add_reader(x1f_b[xslot], ode)
            return Ev(("s_pe", sch.cnt["s_pe"]))

        x2h1_ready_box = [None]

        def rebuild_x2h(x2h_free_ev):
            for m8 in range(8):
                slot = m8 % 2
                ld = sch.dma("sp", (lambda m8=m8, ss=slot: lambda: nc.sync.dma_start(
                    out=xrb[ss][:], in_=x2b_dram[m8]))(),
                    deps=sch.wr(xrb_b[slot]), sem=f"d_rb{slot}")
                sch.set_ready(xrb_b[slot], ld)
                srcs = [xrb[slot][:, ci * 128:(ci + 1) * 128] for ci in range(16)]
                dsts = [x2h_fm[:, ci * 1024 + m8 * 128: ci * 1024 + (m8 + 1) * 128]
                        for ci in range(16)]
                evs = transpose_tiles(srcs, dsts,
                                      sch.rd(xrb_b[slot]) + [x2h_free_ev])
                for e_ in evs:
                    sch.add_reader(xrb_b[slot], e_)
            x2h1_ready_box[0] = [Ev(("s_act", sch.cnt["s_act"])),
                                 Ev(("s_dve", sch.cnt["s_dve"]))]

        ffn_half(0, x2h_ready, mid_cb=rebuild_x2h)
        ffn_half(1, x2h1_ready_box[0])

        sch.op("sp", lambda: None, deps=sch.snapshot())

        sems = {}
        for s in sorted(sch.sem_names):
            sems[s] = E(nc.semaphore(s))

        block = E(nc.Block())

        def runner(engname):
            def run(eng):
                for it in sch.items[engname]:
                    if it[0] == "wait":
                        eng.wait_ge(sems[it[1]], it[2])
                    else:
                        _, fn, s, n = it
                        inst = fn()
                        if s is not None and inst is not None:
                            inst.then_inc(sems[s], n)
            return run

        block.tensor(runner("pe"))
        block.vector(runner("dve"))
        block.scalar(runner("act"))
        block.gpsimd(runner("pool"))
        block.sync(runner("sp"))
        ffn_ctx.close()

    return nc


def _prep_weights(i):
    d = {}
    d["wq"] = np.ascontiguousarray(i["L0_attn_Wq"].T.reshape(16, 128, DM))
    d["wk"] = np.ascontiguousarray(i["L0_attn_Wk"].T.reshape(16, 128, 512))
    d["wv"] = np.ascontiguousarray(i["L0_attn_Wv"].T.reshape(16, 128, 512))
    d["wg"] = np.ascontiguousarray(i["L0_attn_Wg"].T.reshape(16, 128, DM))
    d["wmb"] = np.ascontiguousarray(i["L0_attn_Wmb"].T.reshape(16, 128, DM))
    d["biases"] = np.ascontiguousarray(np.concatenate(
        [i["L0_attn_bq"], i["L0_attn_bg"], i["L0_attn_bmb"],
         i["L0_attn_bk"], i["L0_attn_bv"]]).reshape(1, -1))
    d["mem"] = np.ascontiguousarray(i["L0_attn_memory"].reshape(HQ, NFC, 128, HD))
    d["normf"] = np.ascontiguousarray(i["L0_attn_norm"])
    d["gatew"] = np.ascontiguousarray(i["L0_ffn_gate_w"].T.reshape(16, 128, NE))
    d["sgatew"] = np.ascontiguousarray(i["L0_ffn_sgate_w"].reshape(16, 128).T)
    d["w13"] = np.ascontiguousarray(
        i["L0_ffn_w13"].reshape(NE, 22, 128, 16, 128).transpose(0, 1, 4, 3, 2))
    d["w2"] = np.ascontiguousarray(
        i["L0_ffn_w2"].reshape(NE, 16, 128, 11, 128).transpose(0, 1, 4, 3, 2))
    d["sgu"] = np.ascontiguousarray(
        i["L0_ffn_shared_gate_up_w"].reshape(88, 128, 16, 128).transpose(0, 3, 2, 1))
    d["sdw"] = np.ascontiguousarray(
        i["L0_ffn_shared_down_w"].reshape(16, 128, 44, 128).transpose(0, 3, 2, 1))
    return d


def kernel(**inputs):
    x = np.asarray(inputs["L0_attn_hidden_states"])
    w = _prep_weights({k: np.asarray(v) for k, v in inputs.items()})
    nc = build_program()
    in_maps = []
    for c in range(8):
        m = dict(w)
        m["x"] = np.ascontiguousarray(x[c])
        in_maps.append(m)
    res = run_bass_kernel_spmd(nc, in_maps, core_ids=list(range(8)))
    global _LAST_RES
    _LAST_RES = res
    return np.stack([res.results[c]["out"] for c in range(8)], axis=0)


_LAST_RES = None
